# version 23
# baseline (speedup 1.0000x reference)
import numpy as np
import ml_dtypes
import concourse.bass as bass
import concourse.mybir as mybir
from concourse.bass_utils import run_bass_kernel_spmd

F32 = mybir.dt.float32
BF16 = mybir.dt.bfloat16
OP = mybir.AluOpType
AX = mybir.AxisListType.X
AF = mybir.ActivationFunctionType

SEQ = 2048
D = 1024
NH = 64
II = 2816
T = SEQ // 128
C = D // 128
IC = II // 128
MAGIC = 12582912.0
EPS = 1e-5
HALF = 1024
NCORES = 8
NHEADS = 2 * NH
STOP = "all"


def _finish(nc, x_tok, out_hbm, sd_x, n_):
    with nc.Block() as blk:
        @blk.sync
        def _(sync):
            for t in range(T):
                sync.dma_start(out=out_hbm[t * 128:(t + 1) * 128, :],
                               in_=x_tok.ap()[:, t * D:(t + 1) * D]
                               ).then_inc(sd_x, 16)
            sync.wait_ge(sd_x, n_["dx"] + 16 * T)


def _attn_heads():
    heads = []
    for ih in range(2):
        for h in range(NH):
            chunks = []
            for jc in range(8 if ih == 0 else 16):
                i_lo = max(jc * 128 - ih * HALF, 0)
                diag = (jc * 128) >= ih * HALF
                chunks.append((jc, i_lo, diag))
            heads.append((ih * NH + h, ih, h, chunks))
    return heads


def _nd_pieces(i_lo):
    return [(p, max(i_lo, 512 * p), 512 * (p + 1))
            for p in range(2) if max(i_lo, 512 * p) < 512 * (p + 1)]


def build_nc():
    nc = bass.Bass("TRN2", target_bir_lowering=False,
                   dynamic_dma_scratch_size=512)

    x_hbm = nc.dram_tensor("x", [SEQ, D], F32, kind="ExternalInput").ap()
    wqkv_hbm = nc.dram_tensor("w_qkv", [192, D], F32, kind="ExternalInput").ap()
    wout_hbm = nc.dram_tensor("w_out", [D, NH], F32, kind="ExternalInput").ap()
    clipx_hbm = nc.dram_tensor("clip_x", [1], F32, kind="ExternalInput").ap()
    clipd_hbm = nc.dram_tensor("clip_d", [1], F32, kind="ExternalInput").ap()
    wg_hbm = nc.dram_tensor("w_gate", [II, D], F32, kind="ExternalInput").ap()
    wu_hbm = nc.dram_tensor("w_up", [II, D], F32, kind="ExternalInput").ap()
    wd_hbm = nc.dram_tensor("w_down", [D, II], F32, kind="ExternalInput").ap()
    out_hbm = nc.dram_tensor("out", [SEQ, D], F32, kind="ExternalOutput").ap()

    cc_in1 = nc.dram_tensor("cc_in1", [1, 2048], F32)
    cc_out1 = nc.dram_tensor("cc_out1", [1, 2048], F32)
    cc_in2 = nc.dram_tensor("cc_in2", [1, 2048], F32)
    cc_out2 = nc.dram_tensor("cc_out2", [1, 2048], F32)
    wgT_hbm = nc.dram_tensor("wgT_hbm", [128, IC * D], BF16)
    wuT_hbm = nc.dram_tensor("wuT_hbm", [128, IC * D], BF16)
    wdT_hbm = nc.dram_tensor("wdT_hbm", [128, IC * D], BF16)

    tri_c = nc.inline_tensor(
        (np.arange(128)[None, :] >= np.arange(128)[:, None]
         ).astype(ml_dtypes.bfloat16), name="tri_c")
    idf_c = nc.inline_tensor(np.eye(128, dtype=np.float32), name="idf_c")
    idb_c = nc.inline_tensor(np.eye(128).astype(ml_dtypes.bfloat16), name="idb_c")
    sel_c = nc.inline_tensor(
        np.kron(np.eye(64), np.ones((1, 128))).astype(ml_dtypes.bfloat16),
        name="sel_c")
    ones_c = nc.inline_tensor(np.ones((1, 128), dtype=np.float32), name="ones_c")
    zero_c = nc.inline_tensor(np.zeros((1, 2048), dtype=np.float32),
                              name="zero_c")

    from contextlib import ExitStack
    with ExitStack() as ctx:
        en = ctx.enter_context

        x_tok = en(nc.sbuf_tensor("x_tok", [128, T * D], F32))
        tri = en(nc.sbuf_tensor("tri", [128, 128], BF16))
        idf = en(nc.sbuf_tensor("idf", [128, 128], F32))
        idb = en(nc.sbuf_tensor("idb", [128, 128], BF16))
        ones1 = en(nc.sbuf_tensor("ones1", [1, 128], F32))
        clipx_r = en(nc.sbuf_tensor("clipx_r", [128, 1], F32))
        clipd_r = en(nc.sbuf_tensor("clipd_r", [128, 1], F32))
        mu_all = en(nc.sbuf_tensor("mu_all", [128, T], F32))
        rstd_all = en(nc.sbuf_tensor("rstd_all", [128, T], F32))
        amax_run = en(nc.sbuf_tensor("amax_run", [128, 1], F32))
        sc_sx = en(nc.sbuf_tensor("sc_sx", [128, 1], F32))
        sc_rx = en(nc.sbuf_tensor("sc_rx", [128, 1], F32))
        sc_sh = en(nc.sbuf_tensor("sc_sh", [128, 1], F32))
        sc_rh = en(nc.sbuf_tensor("sc_rh", [128, 1], F32))
        gmax_r = en(nc.sbuf_tensor("gmax_r", [128, 1], F32))
        tmp_a = en(nc.sbuf_tensor("tmp_a", [128, 1], F32))
        tmp_b = en(nc.sbuf_tensor("tmp_b", [128, 1], F32))
        tmp_c = en(nc.sbuf_tensor("tmp_c", [128, 1], F32))
        tmp_d = en(nc.sbuf_tensor("tmp_d", [128, 1], F32))
        tmp_e = en(nc.sbuf_tensor("tmp_e", [128, 1], F32))
        tmp_f = en(nc.sbuf_tensor("tmp_f", [128, 1], F32))
        amat = en(nc.sbuf_tensor("amat", [128, T], F32))
        bmat = en(nc.sbuf_tensor("bmat", [128, T], F32))
        scr1 = en(nc.sbuf_tensor("scr1", [128, 1024], F32))
        bnst = en(nc.sbuf_tensor("bnst", [128, 12], F32))
        muvar = en(nc.sbuf_tensor("muvar", [128, 2 * T], F32))
        xmax_a = en(nc.sbuf_tensor("xmax_a", [128, T], F32))
        xmin_a = en(nc.sbuf_tensor("xmin_a", [128, T], F32))
        dev_b = en(nc.sbuf_tensor("dev_b", [128, T], F32))
        sqv = en(nc.sbuf_tensor("sqv", [128, T], F32))
        magic_r = en(nc.sbuf_tensor("magic_r", [128, 1], F32))
        eps_r = en(nc.sbuf_tensor("eps_r", [128, 1], F32))

        sd_x = en(nc.semaphore("sd_x"))
        sd_xa = (en(nc.semaphore("sd_xa0")), en(nc.semaphore("sd_xa1")))
        sd_xb = (en(nc.semaphore("sd_xb0")), en(nc.semaphore("sd_xb1")))
        s_P = en(nc.semaphore("s_P"))
        sd_w = en(nc.semaphore("sd_w"))
        s_T = en(nc.semaphore("s_T"))
        s_S = en(nc.semaphore("s_S"))
        s_V = en(nc.semaphore("s_V"))
        s_cc = en(nc.semaphore("s_cc"))
        sd_p0 = en(nc.semaphore("sd_p0"))
        sd_p1 = en(nc.semaphore("sd_p1"))
        sd_wg0 = en(nc.semaphore("sd_wg0"))
        sd_wg1 = en(nc.semaphore("sd_wg1"))
        sd_wu0 = en(nc.semaphore("sd_wu0"))
        sd_wu1 = en(nc.semaphore("sd_wu1"))
        sd_wg = (sd_wg0, sd_wg1)
        sd_wu = (sd_wu0, sd_wu1)
        sd_tg = (en(nc.semaphore("sd_tg0")), en(nc.semaphore("sd_tg1")))
        sd_tu = (en(nc.semaphore("sd_tu0")), en(nc.semaphore("sd_tu1")))
        sd_tq = (en(nc.semaphore("sd_tq0")), en(nc.semaphore("sd_tq1")))
        sd_td = (en(nc.semaphore("sd_td0")), en(nc.semaphore("sd_td1")))
        sd_tl = (en(nc.semaphore("sd_tl0")), en(nc.semaphore("sd_tl1")))
        sd_ts = (en(nc.semaphore("sd_ts0")), en(nc.semaphore("sd_ts1")))
        sd_tx = (en(nc.semaphore("sd_tx0")), en(nc.semaphore("sd_tx1")))
        sd_wd = en(nc.semaphore("sd_wd"))
        sd_wr = (en(nc.semaphore("sd_wr0")), en(nc.semaphore("sd_wr1")))
        sd_wt = (en(nc.semaphore("sd_wt0")), en(nc.semaphore("sd_wt1")))
        sd_wb = (en(nc.semaphore("sd_wb0")), en(nc.semaphore("sd_wb1")))
        s_wq = en(nc.semaphore("s_wq"))
        n_ = {"dx": 0, "dw": 0, "T": 0, "S": 0, "V": 0, "cc": 0, "P": 0}

        with ExitStack() as actx:
            an = actx.enter_context
            sel = an(nc.sbuf_tensor("sel", [64, NH * 128], BF16))
            wqkvT = an(nc.sbuf_tensor("wqkvT", [128, C * 192], BF16))
            woutT = an(nc.sbuf_tensor("woutT", [64, D], BF16))
            qb = an(nc.sbuf_tensor("qb", [64, SEQ], BF16))
            kT = an(nc.sbuf_tensor("kT", [128, T * NH], F32))
            vT_p = an(nc.sbuf_tensor("vT_p", [128, T * 128], BF16))
            negm = an(nc.sbuf_tensor("negm", [128, NH], F32))
            negm_row = an(nc.sbuf_tensor("negm_row", [1, NH], F32))
            E_buf = an(nc.sbuf_tensor("E_buf", [128, 6 * HALF], BF16))
            pair_sb = an(nc.sbuf_tensor("pair_sb", [2, 2 * HALF], F32))
            na_sb = an(nc.sbuf_tensor("na_sb", [64, SEQ], F32))
            da_sb = an(nc.sbuf_tensor("da_sb", [64, SEQ], F32))
            attn_o = qb
            ln_tile = an(nc.sbuf_tensor("ln_tile", [128, 2 * D], BF16))
            ln1T = an(nc.sbuf_tensor("ln1T", [128, C * SEQ], BF16))

            with (
                nc.sbuf_tensor("wq_a", [128, D], F32) as wq_a,
                nc.sbuf_tensor("wq_b", [64, D], F32) as wq_b,
                nc.sbuf_tensor("wo_raw", [128, C * NH], F32) as wo_raw,
                nc.psum_tensor("p_t0", [128, 2048], F32) as p_t0,
                nc.Block() as blk,
            ):
                @blk.sync
                def _(sync):
                    for t in range(0, T, 2):
                        i2 = t // 2
                        if i2 >= 2:
                            sync.wait_ge(sd_xa[i2 % 2], 16 * (i2 // 2))
                        sync.dma_start(out=x_tok.ap()[:, t * D:(t + 1) * D],
                                       in_=x_hbm[t * 128:(t + 1) * 128, :]
                                       ).then_inc(sd_xa[i2 % 2], 16)

                @blk.scalar
                def _(g):
                    for dst, src in ((tri, tri_c), (idf, idf_c), (idb, idb_c),
                                     (ones1, ones_c), (sel, sel_c)):
                        g.dma_start(out=dst[:, :], in_=src.ap()).then_inc(sd_w, 16)
                    g.dma_start(out=clipx_r[:, :],
                                in_=clipx_hbm[None, :].to_broadcast([128, 1])
                                ).then_inc(sd_w, 16)
                    g.dma_start(out=clipd_r[:, :],
                                in_=clipd_hbm[None, :].to_broadcast([128, 1])
                                ).then_inc(sd_w, 16)
                    g.dma_start(out=wq_a[:, :], in_=wqkv_hbm[0:128, :]
                                ).then_inc(sd_w, 16)
                    g.dma_start(out=wq_b[:, :], in_=wqkv_hbm[128:192, :]
                                ).then_inc(sd_w, 16)
                    g.dma_start(out=wo_raw.ap().rearrange("p (c h) -> p c h", c=C),
                                in_=wout_hbm.rearrange("(c p) h -> p c h", p=128)
                                ).then_inc(sd_w, 16)
                    g.dma_start(out=cc_in1.ap(), in_=zero_c.ap()
                                ).then_inc(sd_w, 16)
                    g.dma_start(out=cc_in2.ap(), in_=zero_c.ap()
                                ).then_inc(sd_w, 16)
                    for t in range(1, T, 2):
                        i2 = t // 2
                        if i2 >= 2:
                            g.wait_ge(sd_xb[i2 % 2], 16 * (i2 // 2))
                        g.dma_start(out=x_tok.ap()[:, t * D:(t + 1) * D],
                                    in_=x_hbm[t * 128:(t + 1) * 128, :]
                                    ).then_inc(sd_xb[i2 % 2], 16)
                n_["dw"] += 12 * 16

                items = ([("qk", kc) for kc in range(C)]
                         + [("v", kc) for kc in range(C)]
                         + [("wo", kc) for kc in range(C)])
                if STOP == "load":
                    items = []
                T0, V0 = n_["T"], n_["V"]

                @blk.tensor
                def _(tensor):
                    tensor.wait_ge(sd_w, n_["dw"])
                    for i, (kind, kc) in enumerate(items):
                        slot = (i % 4) * 512
                        if i >= 4:
                            tensor.wait_ge(s_V, V0 + (i - 4) + 1)
                        if kind == "qk":
                            tensor.transpose(p_t0[:, slot:slot + 128],
                                             wq_a[:, kc * 128:(kc + 1) * 128],
                                             idf[:, :]).then_inc(s_T, 1)
                        elif kind == "v":
                            tensor.transpose(p_t0[:, slot:slot + 64],
                                             wq_b[:, kc * 128:(kc + 1) * 128],
                                             idf[0:64, 0:64]).then_inc(s_T, 1)
                        else:
                            tensor.transpose(p_t0[0:64, slot:slot + 128],
                                             wo_raw[:, kc * NH:(kc + 1) * NH],
                                             idf[:, :]).then_inc(s_T, 1)
                n_["T"] += len(items)

                @blk.vector
                def _(v):
                    for i, (kind, kc) in enumerate(items):
                        slot = (i % 4) * 512
                        v.wait_ge(s_T, T0 + i + 1)
                        if kind == "qk":
                            v.tensor_copy(wqkvT.ap()[:, kc * 192:kc * 192 + 128],
                                          p_t0[:, slot:slot + 128]).then_inc(s_V, 1)
                        elif kind == "v":
                            v.tensor_copy(
                                wqkvT.ap()[:, kc * 192 + 128:kc * 192 + 192],
                                p_t0[:, slot:slot + 64]).then_inc(s_V, 1)
                        else:
                            v.tensor_copy(woutT.ap()[:, kc * 128:(kc + 1) * 128],
                                          p_t0[0:64, slot:slot + 128]).then_inc(s_V, 1)
                    v.memset(vT_p[:, :], 1.0)
                    v.memset(eps_r[:, :], EPS)
                    v.memset(magic_r[:, :], MAGIC).then_inc(s_V, 1)
                n_["V"] += len(items) + 1

            v_after_p0 = n_["V"]
            if STOP in ("p0", "load"):
                _finish(nc, x_tok, out_hbm, sd_x, n_)
                return nc

            with (
                nc.sbuf_tensor("lnstage", [128, 2 * D], BF16) as lnstage,
                nc.Block() as blk,
            ):
                V0, S0, T0 = n_["V"], n_["S"], n_["T"]
                pv, pl = {}, {}
                _c = 0
                _c += 1
                pv[0] = V0 + _c
                for t in range(T):
                    if t + 1 < T:
                        _c += 1
                        pv[t + 1] = V0 + _c
                    _c += 1
                    pl[t] = V0 + _c

                def pos_var(t):
                    return pv[t]

                def pos_ln(t):
                    return pl[t]

                tl_after = {}
                tlc = [0, 0]
                for t in range(T):
                    tlc[t % 2] += 16
                    tl_after[t] = tlc[t % 2]

                def xwait(v, t):
                    i2 = t // 2
                    sem = sd_xa if t % 2 == 0 else sd_xb
                    v.wait_ge(sem[i2 % 2], 16 * (i2 // 2 + 1))

                @blk.vector
                def _(v):
                    def stats(t):
                        xt = x_tok.ap()[:, t * D:(t + 1) * D]
                        xwait(v, t)
                        v.drain()
                        v.bn_stats(bnst[:, 0:6], xt[:, 0:512])
                        v.bn_stats(bnst[:, 6:12], xt[:, 512:1024])
                        v.drain()
                        v.bn_aggr(muvar.ap()[:, 2 * t:2 * t + 2], bnst[:, :]
                                  ).then_inc(s_V, 1)
                    stats(0)
                    for t in range(T):
                        if t + 1 < T:
                            stats(t + 1)
                        xt = x_tok.ap()[:, t * D:(t + 1) * D]
                        v.wait_ge(s_S, S0 + t + 1)
                        v.drain()
                        v.reciprocal(rstd_all.ap()[:, t:t + 1],
                                     sqv.ap()[:, t:t + 1])
                        v.drain()
                        v.tensor_scalar(tmp_c[:, :],
                                        muvar.ap()[:, 2 * t:2 * t + 1],
                                        rstd_all.ap()[:, t:t + 1], -1.0,
                                        OP.mult, OP.mult)
                        v.drain()
                        if t >= 2:
                            v.wait_ge(sd_tl[t % 2], tl_after[t - 2])
                        v.tensor_scalar(
                            ln_tile.ap()[:, (t % 2) * D:(t % 2 + 1) * D], xt,
                            rstd_all.ap()[:, t:t + 1], tmp_c[:, :],
                            OP.mult, OP.add).then_inc(s_V, 1)
                n_["V"] += 2 * T

                @blk.scalar
                def _(s):
                    for t in range(T):
                        s.wait_ge(s_V, pos_var(t))
                        s.activation(sqv.ap()[:, t:t + 1],
                                     muvar.ap()[:, 2 * t + 1:2 * t + 2],
                                     AF.Sqrt, bias=eps_r[:, :], scale=1.0
                                     ).then_inc(s_S, 1)
                n_["S"] += T

                @blk.sync
                def _(sync):
                    for t in range(T):
                        sync.wait_ge(s_V, pos_ln(t))
                        if t >= 2:
                            sync.wait_ge(sd_ts[t % 2], 16 * (t // 2))
                        sync.dma_start_transpose(
                            lnstage.ap()[:, (t % 2) * D:(t % 2 + 1) * D
                                         ].rearrange("p (j c) -> p j c", c=128),
                            ln_tile.ap()[:, (t % 2) * D:(t % 2 + 1) * D
                                         ].rearrange("p (j c) -> p j c", c=128)
                            ).then_inc(sd_tl[t % 2], 16)
                        sync.wait_ge(sd_tl[t % 2], 16 * (t // 2 + 1))
                        sync.dma_start(
                            out=ln1T.ap().rearrange("p (j s) -> p j s",
                                                    s=SEQ)[:, :,
                                                           t * 128:(t + 1) * 128],
                            in_=lnstage.ap()[:, (t % 2) * D:(t % 2 + 1) * D
                                             ].rearrange("p (j c) -> p j c",
                                                         c=128)
                            ).then_inc(sd_ts[t % 2], 16)

            if STOP == "p1":
                _finish(nc, x_tok, out_hbm, sd_x, n_)
                return nc

            with (
                nc.sbuf_tensor("qk_fm", [128, SEQ], F32) as qk_fm,
                nc.sbuf_tensor("v_fm", [64, SEQ], F32) as v_fm,
                nc.psum_tensor("p_qkv", [128, SEQ], F32) as p_qkv,
                nc.psum_tensor("p_t2", [128, 1024], F32) as p_t2,
                nc.psum_tensor("p_sm", [1, 128], F32) as p_sm,
                nc.psum_tensor("p_sm2", [128, 64], F32) as p_sm2,
                nc.Block() as blk,
            ):
                T0, V0 = n_["T"], n_["V"]

                @blk.tensor
                def _(tensor):
                    tensor.wait_ge(s_V, v_after_p0)
                    for piece in range(4):
                        sl = slice(piece * 512, (piece + 1) * 512)
                        tensor.wait_ge(sd_ts[0], 16 * (2 * piece + 2))
                        tensor.wait_ge(sd_ts[1], 16 * (2 * piece + 2))
                        for kc in range(C):
                            mm = tensor.matmul(
                                p_qkv[:, sl],
                                wqkvT.ap()[:, kc * 192:kc * 192 + 128],
                                ln1T.ap()[:, kc * SEQ + piece * 512:
                                          kc * SEQ + (piece + 1) * 512],
                                start=(kc == 0), stop=(kc == C - 1))
                        mm.then_inc(s_T, 1)
                    for piece in range(4):
                        sl = slice(piece * 512, (piece + 1) * 512)
                        tensor.wait_ge(s_V, V0 + piece + 1)
                        for kc in range(C):
                            mm = tensor.matmul(
                                p_qkv[0:64, sl],
                                wqkvT.ap()[:, kc * 192 + 128:kc * 192 + 192],
                                ln1T.ap()[:, kc * SEQ + piece * 512:
                                          kc * SEQ + (piece + 1) * 512],
                                start=(kc == 0), stop=(kc == C - 1))
                        mm.then_inc(s_T, 1)
                    for i in range(32):
                        jc = i % 16
                        slot = (i % 2) * 512
                        if i == 0:
                            tensor.wait_ge(s_V, V0 + 4)
                        if i == 16:
                            tensor.wait_ge(s_V, V0 + 8)
                        if i >= 2:
                            tensor.wait_ge(s_V, V0 + 8 + (i - 2) + 1)
                        if i < 16:
                            tensor.transpose(p_t2[:, slot:slot + 64],
                                             qk_fm.ap()[64:128,
                                                        jc * 128:(jc + 1) * 128],
                                             idf[64:128, 64:128]).then_inc(s_T, 1)
                        else:
                            tensor.transpose(p_t2[:, slot:slot + 64],
                                             v_fm.ap()[:, jc * 128:(jc + 1) * 128],
                                             idf[0:64, 0:64]).then_inc(s_T, 1)
                    tensor.wait_ge(s_V, V0 + 41)
                    tensor.transpose(p_sm[0:1, 0:64], tmp_c[0:64, 0:1],
                                     idf[0:64, 0:64]).then_inc(s_T, 1)
                    tensor.wait_ge(s_V, V0 + 42)
                    tensor.matmul(p_sm2[:, :], ones1[:, :], negm_row[:, :],
                                  start=True, stop=True).then_inc(s_T, 1)
                n_["T"] += 4 + 4 + 32 + 2

                @blk.vector
                def _(v):
                    v.wait_ge(sd_ts[0], 16 * (T // 2))
                    v.wait_ge(sd_ts[1], 16 * (T // 2))
                    for piece in range(4):
                        sl = slice(piece * 512, (piece + 1) * 512)
                        v.wait_ge(s_T, T0 + piece + 1)
                        v.tensor_copy(qk_fm.ap()[:, sl], p_qkv[:, sl])
                        v.tensor_copy(qb.ap()[:, sl], p_qkv[0:64, sl]
                                      ).then_inc(s_V, 1)
                    for piece in range(4):
                        sl = slice(piece * 512, (piece + 1) * 512)
                        v.wait_ge(s_T, T0 + 4 + piece + 1)
                        v.tensor_copy(v_fm.ap()[:, sl], p_qkv[0:64, sl]
                                      ).then_inc(s_V, 1)
                    for i in range(32):
                        jc = i % 16
                        slot = (i % 2) * 512
                        v.wait_ge(s_T, T0 + 8 + i + 1)
                        if i < 16:
                            v.tensor_scalar(kT.ap()[:, jc * 64:(jc + 1) * 64],
                                            p_t2[:, slot:slot + 64], 0.125,
                                            None, OP.mult).then_inc(s_V, 1)
                        else:
                            v.tensor_copy(
                                vT_p.ap()[:, jc * 128 + 1:jc * 128 + 128:2],
                                p_t2[:, slot:slot + 64]).then_inc(s_V, 1)
                    v.drain()
                    v.tensor_reduce(tmp_a[0:64, :], qk_fm.ap()[0:64, :], AX,
                                    OP.max, apply_absolute_value=True)
                    v.tensor_reduce(tmp_b[0:64, :], qk_fm.ap()[64:128, :], AX,
                                    OP.max, apply_absolute_value=True)
                    v.drain()
                    v.tensor_mul(tmp_c[0:64, :], tmp_a[0:64, :], tmp_b[0:64, :])
                    v.drain()
                    v.tensor_scalar(tmp_c[0:64, :], tmp_c[0:64, :], -0.125,
                                    None, OP.mult).then_inc(s_V, 1)
                    v.wait_ge(s_T, T0 + 41)
                    v.tensor_copy(negm_row[:, :], p_sm[0:1, 0:64]
                                  ).then_inc(s_V, 1)
                    v.wait_ge(s_T, T0 + 42)
                    v.tensor_copy(negm[:, :], p_sm2[:, :]).then_inc(s_V, 1)
                n_["V"] += 8 + 32 + 3

            if STOP == "p2":
                _finish(nc, x_tok, out_hbm, sd_x, n_)
                return nc

            v_after_p2 = n_["V"]
            heads = _attn_heads()
            T0, S0, V0, DX0 = n_["T"], n_["S"], n_["V"], n_["dx"]
            qrep_done, nd_chunk_done, nd_head_done = {}, {}, {}
            exp_pos, exp_head_done, mask_pos, pairv_pos = {}, {}, {}, {}
            tc = T0 + 2 + 2
            qrep_done[0] = T0 + 2
            qrep_done[1] = T0 + 4
            k = 0
            for (nn2, ih, h, chunks) in heads:
                for (jc, i_lo, diag) in chunks:
                    tc += len(_nd_pieces(i_lo))
                    nd_chunk_done[k] = tc
                    k += 1
                nd_head_done[nn2] = tc
                if nn2 + 2 < NHEADS:
                    tc += 2
                    qrep_done[nn2 + 2] = tc
            k = 0
            for (nn2, ih, h, chunks) in heads:
                for _ in chunks:
                    exp_pos[k] = S0 + k + 1
                    k += 1
                exp_head_done[nn2] = S0 + k
            vc = V0
            k = 0
            for (nn2, ih, h, chunks) in heads:
                if nn2 >= 1:
                    vc += 1
                    pairv_pos[nn2 - 1] = vc
                for (jc, i_lo, diag) in chunks:
                    if diag:
                        vc += 1
                        mask_pos[k] = vc
                    k += 1
            vc += 1
            pairv_pos[NHEADS - 1] = vc
            pair_dma = {m: 32 * (m // 2 + 1) for m in range(NHEADS)}
            sd_p = (sd_p0, sd_p1)
            NCHUNK = k

            wblocks = []
            for i_ in range(IC):
                wblocks.append(("g", i_))
                wblocks.append(("u", i_))
            for i_ in range(C):
                wblocks.append(("d", i_))
            NWB = len(wblocks)

            def wb_ntiles(p):
                return IC if wblocks[p][0] == "d" else C

            def w_off(p):
                return 2 * D if wblocks[p][0] == "d" else (p % 2) * D

            wt_after, wb_after = {}, {}
            _wtc, _wbc = [0, 0], [0, 0]
            for p in range(NWB):
                _wtc[p % 2] += 16
                wt_after[p] = _wtc[p % 2]
                _wbc[p % 2] += 16
                wb_after[p] = _wbc[p % 2]
            n_["wb0"], n_["wb1"] = _wbc[0], _wbc[1]
            heada_of, raw_head, piece_heads = {}, {}, {}
            for p in range(NWB):
                if wblocks[p][0] == "d":
                    nd_ = p - (2 * IC)
                    heada_of[p] = 4 * IC + 4 * nd_
                    piece_heads[p] = [heada_of[p] + 1 + i for i in range(3)]
                    raw_head[p] = heada_of[p]
                else:
                    heada_of[p] = 2 * p
                    piece_heads[p] = [2 * p + 1]
                    raw_head[p] = 0 if p < 2 else 2 * p
            xbar_head = {p: piece_heads[p][-1] + 1 for p in range(NWB)}
            wb_head = {p: xbar_head[p] + 1 for p in range(NWB)}
            quanta_at, quantb_at = {}, {}
            raw_at, xbar_at, wb_at = {}, {}, {}
            for p in range(NWB):
                quanta_at.setdefault(heada_of[p], []).append(p)
                wid_ = II if wblocks[p][0] == "d" else D
                offs = list(range(0, wid_, 1024))
                for i, off in enumerate(offs):
                    quantb_at.setdefault(piece_heads[p][i], []).append(
                        (p, off, i == 0, i == len(offs) - 1))
                raw_at.setdefault(raw_head[p], []).append(p)
                xbar_at.setdefault(xbar_head[p], []).append(p)
                wb_at.setdefault(wb_head[p], []).append(p)

            def emit_wraw(sc, p):
                kind, wi = wblocks[p]
                hh2 = p % 2
                wid = II if kind == "d" else D
                if kind == "d" and p >= 1 and wblocks[p - 1][0] == "d":
                    sc.wait_ge(s_wq, p)
                elif p >= 2:
                    sc.wait_ge(s_wq, p - 1)
                srcw = {"g": wg_hbm, "u": wu_hbm, "d": wd_hbm}[kind]
                obase = w_off(p)
                sc.dma_start(out=wrawW.ap()[:, obase:obase + wid],
                             in_=srcw[wi * 128:(wi + 1) * 128, 0:wid]
                             ).then_inc(sd_wr[hh2], 16)

            def emit_wquant_a(v, p):
                kind, wi = wblocks[p]
                hh2 = p % 2
                wid = II if kind == "d" else D
                rbase = w_off(p)
                v.wait_ge(sd_wr[hh2], 16 * (p // 2 + 1))
                v.drain()
                v.tensor_reduce(wscW.ap()[:, 0:1],
                                wrawW.ap()[:, rbase:rbase + wid], AX,
                                OP.max, apply_absolute_value=True)
                v.drain()
                v.tensor_scalar(wscW.ap()[:, 0:1], wscW.ap()[:, 0:1],
                                1.0 / 127.0, None, OP.mult)
                v.drain()
                v.reciprocal(wscW.ap()[:, 1:2], wscW.ap()[:, 0:1])
                v.drain()

            def emit_wquant_b(v, p, off, first, last):
                kind, wi = wblocks[p]
                wid = II if kind == "d" else D
                rbase = w_off(p)
                if first:
                    if kind == "d":
                        if wblocks[p - 1][0] == "d":
                            v.wait_ge(sd_wt[(p - 1) % 2], wt_after[p - 1])
                    elif p >= 2:
                        v.wait_ge(sd_wt[p % 2], wt_after[p - 2])
                w2 = min(1024, wid - off)
                v.tensor_scalar(scr1[:, 0:w2],
                                wrawW.ap()[:, rbase + off:rbase + off + w2],
                                wscW.ap()[:, 1:2], MAGIC,
                                OP.mult, OP.add)
                v.drain()
                ts = v.tensor_scalar(
                    wqW.ap()[:, rbase + off:rbase + off + w2],
                    scr1[:, 0:w2], MAGIC,
                    wscW.ap()[:, 0:1], OP.subtract, OP.mult)
                if last:
                    ts.then_inc(s_wq, 1)
                v.drain()

            def emit_wwb(sync, p):
                kind, wi = wblocks[p]
                hh2 = p % 2
                ob = w_off(p)
                sync.wait_ge(sd_wt[hh2], wt_after[p])
                if kind == "g":
                    sync.dma_start(out=wgT_hbm.ap()[:, wi * D:(wi + 1) * D],
                                   in_=wtW.ap()[:, ob:ob + D]
                                   ).then_inc(sd_wb[hh2], 16)
                elif kind == "u":
                    sync.dma_start(out=wuT_hbm.ap()[:, wi * D:(wi + 1) * D],
                                   in_=wtW.ap()[:, ob:ob + D]
                                   ).then_inc(sd_wb[hh2], 16)
                else:
                    sync.dma_start(
                        out=wdT_hbm.ap().rearrange(
                            "p (j c) -> p j c",
                            c=D)[:, :, wi * 128:(wi + 1) * 128],
                        in_=wtW.ap()[:, ob:ob + IC * 128].rearrange(
                            "p (j c) -> p j c", c=128)
                        ).then_inc(sd_wb[hh2], 16)

            def emit_wxbar(sync, p):
                kind, wi = wblocks[p]
                hh2 = p % 2
                ob = w_off(p)
                nt = wb_ntiles(p)
                sync.wait_ge(s_wq, p + 1)
                if kind == "d":
                    if wblocks[p - 1][0] == "d":
                        sync.wait_ge(sd_wb[(p - 1) % 2], wb_after[p - 1])
                elif p >= 2:
                    sync.wait_ge(sd_wb[hh2], wb_after[p - 2])
                sync.dma_start_transpose(
                    wtW.ap()[:, ob:ob + nt * 128].rearrange(
                        "p (j c) -> p j c", c=128),
                    wqW.ap()[:, ob:ob + nt * 128].rearrange(
                        "p (j c) -> p j c", c=128)
                    ).then_inc(sd_wt[hh2], 16)

            with (
                nc.sbuf_tensor("wrawW", [128, 2 * D + II], F32) as wrawW,
                nc.sbuf_tensor("wqW", [128, 2 * D + II], BF16) as wqW,
                nc.sbuf_tensor("wtW", [128, 2 * D + II], BF16) as wtW,
                nc.sbuf_tensor("wscW", [128, 2], F32) as wscW,
                nc.psum_tensor("p_qr0", [128, HALF], F32) as p_qr0,
                nc.psum_tensor("p_qr1", [128, HALF], F32) as p_qr1,
                nc.psum_tensor("p_nd0", [2, HALF], F32) as p_nd0,
                nc.psum_tensor("p_nd1", [2, HALF], F32) as p_nd1,
                nc.Block() as blk,
            ):
                p_qr = (p_qr0, p_qr1)
                p_nd = (p_nd0, p_nd1)

                def emit_qrep(tensor, m):
                    _, ihm, hm, _ = heads[m]
                    if m >= 2:
                        tensor.wait_ge(s_S, exp_head_done[m - 2])
                    for half2 in range(2):
                        tensor.matmul(
                            p_qr[m % 2][:, half2 * 512:(half2 + 1) * 512],
                            sel.ap()[:, hm * 128:(hm + 1) * 128],
                            qb.ap()[:, ihm * HALF + half2 * 512:
                                    ihm * HALF + (half2 + 1) * 512],
                            start=True, stop=True).then_inc(s_T, 1)

                @blk.tensor
                def _(tensor):
                    tensor.wait_ge(s_V, v_after_p2)
                    emit_qrep(tensor, 0)
                    emit_qrep(tensor, 1)
                    kk = 0
                    for (m, ih, h, chunks) in heads:
                        for ci, (jc, i_lo, diag) in enumerate(chunks):
                            if ci == 0 and m >= 2:
                                tensor.wait_ge(s_V, pairv_pos[m - 2])
                            if diag:
                                tensor.wait_ge(s_V, mask_pos[kk])
                            else:
                                tensor.wait_ge(s_S, exp_pos[kk])
                            par = kk % 6
                            last_ci = {}
                            for ci2, (_, il2, _) in enumerate(chunks):
                                for (p2, _, _) in _nd_pieces(il2):
                                    last_ci[p2] = ci2
                            for (p, lo, hi) in _nd_pieces(i_lo):
                                tensor.matmul(
                                    p_nd[m % 2][0:2, lo:hi],
                                    vT_p.ap()[:, jc * 128 + 2 * h:
                                              jc * 128 + 2 * h + 2],
                                    E_buf.ap()[:, par * HALF + lo:
                                               par * HALF + hi],
                                    start=(ci == 0),
                                    stop=(ci == last_ci[p])).then_inc(s_T, 1)
                            kk += 1
                        if m + 2 < NHEADS:
                            emit_qrep(tensor, m + 2)
                n_["T"] = tc

                @blk.scalar
                def _(s):
                    s.wait_ge(s_V, v_after_p2)
                    kk = 0
                    for (m, ih, h, chunks) in heads:
                        for p_ in raw_at.get(m, ()):
                            emit_wraw(s, p_)
                        for (jc, i_lo, diag) in chunks:
                            par = kk % 6
                            w = qrep_done[m]
                            if kk >= 6:
                                w = max(w, nd_chunk_done[kk - 6])
                            s.wait_ge(s_T, w)
                            s.activation(
                                E_buf.ap()[:, par * HALF + i_lo:(par + 1) * HALF],
                                p_qr[m % 2][:, i_lo:HALF],
                                AF.Exp,
                                bias=negm.ap()[:, h:h + 1],
                                scale=kT.ap()[:, jc * 64 + h:jc * 64 + h + 1]
                                ).then_inc(s_S, 1)
                            kk += 1
                n_["S"] += NCHUNK

                @blk.vector
                def _(v):
                    kk = 0
                    for (m, ih, h, chunks) in heads:
                        if m >= 1:
                            mp = m - 1
                            v.wait_ge(s_T, nd_head_done[mp])
                            if mp >= 2:
                                v.wait_ge(sd_p[mp % 2], pair_dma[mp - 2])
                            v.tensor_copy(
                                pair_sb.ap()[:, (mp % 2) * HALF:
                                             (mp % 2 + 1) * HALF],
                                p_nd[mp % 2][:, :]).then_inc(s_V, 1)
                        for (jc, i_lo, diag) in chunks:
                            if diag:
                                par = kk % 6
                                v.wait_ge(s_S, exp_pos[kk])
                                v.tensor_mul(
                                    E_buf.ap()[:, par * HALF + i_lo:
                                               par * HALF + i_lo + 128],
                                    E_buf.ap()[:, par * HALF + i_lo:
                                               par * HALF + i_lo + 128],
                                    tri[:, :]).then_inc(s_V, 1)
                            kk += 1
                        for p_ in quanta_at.get(m, ()):
                            emit_wquant_a(v, p_)
                        for (p_, off_, fi_, la_) in quantb_at.get(m, ()):
                            emit_wquant_b(v, p_, off_, fi_, la_)
                    mp = NHEADS - 1
                    v.wait_ge(s_T, nd_head_done[mp])
                    v.wait_ge(sd_p[mp % 2], pair_dma[mp - 2])
                    v.tensor_copy(pair_sb.ap()[:, (mp % 2) * HALF:
                                               (mp % 2 + 1) * HALF],
                                  p_nd[mp % 2][:, :]).then_inc(s_V, 1)
                n_["V"] = vc

                @blk.sync
                def _(sync):
                    for (m, ih, h, chunks) in heads:
                        for p_ in wb_at.get(m, ()):
                            emit_wwb(sync, p_)
                        for p_ in xbar_at.get(m, ()):
                            emit_wxbar(sync, p_)
                        sync.wait_ge(s_V, pairv_pos[m])
                        if m == NHEADS - 1:
                            sync.wait_ge(sd_wt[0], _wtc[0])
                            sync.wait_ge(sd_wt[1], _wtc[1])
                            sync.wait_ge(sd_wb[0], _wbc[0])
                            sync.wait_ge(sd_wb[1], _wbc[1])
                        sec = (m % 2) * HALF
                        sync.dma_start(
                            out=da_sb.ap()[h:h + 1, ih * HALF:(ih + 1) * HALF],
                            in_=pair_sb.ap()[0:1, sec:sec + HALF]
                            ).then_inc(sd_p[m % 2], 16)
                        sync.dma_start(
                            out=na_sb.ap()[h:h + 1, ih * HALF:(ih + 1) * HALF],
                            in_=pair_sb.ap()[1:2, sec:sec + HALF]
                            ).then_inc(sd_p[m % 2], 16)

            nc.all_engine_barrier()
            if STOP == "p3":
                _finish(nc, x_tok, out_hbm, sd_x, n_)
                return nc

            with (
                nc.psum_tensor("p_w", [128, 1536], F32) as p_w,
                nc.Block() as blk,
            ):
                T0, V0 = n_["T"], n_["V"]

                @blk.vector
                def _(v):
                    v.wait_ge(sd_p0, 32 * (NHEADS // 2))
                    v.wait_ge(sd_p1, 32 * (NHEADS // 2))
                    v.reciprocal(da_sb[:, :], da_sb[:, :])
                    v.drain()
                    v.tensor_mul(attn_o[:, :], na_sb[:, :], da_sb[:, :]
                                 ).then_inc(s_V, 1)
                    for idx in range(32):
                        t, piece = divmod(idx, 2)
                        v.wait_ge(s_T, T0 + idx + 1)
                        sl = slice((idx % 3) * 512, (idx % 3) * 512 + 512)
                        xsl = slice(t * D + piece * 512, t * D + piece * 512 + 512)
                        v.tensor_add(x_tok.ap()[:, xsl], x_tok.ap()[:, xsl],
                                     p_w[:, sl]).then_inc(s_V, 1)
                n_["V"] += 33

                @blk.tensor
                def _(tensor):
                    tensor.wait_ge(s_V, V0 + 1)
                    for idx in range(32):
                        t, piece = divmod(idx, 2)
                        if idx >= 3:
                            tensor.wait_ge(s_V, V0 + 1 + (idx - 3) + 1)
                        sl = slice((idx % 3) * 512, (idx % 3) * 512 + 512)
                        tensor.matmul(p_w[:, sl],
                                      attn_o.ap()[:, t * 128:(t + 1) * 128],
                                      woutT.ap()[:, piece * 512:(piece + 1) * 512],
                                      start=True, stop=True).then_inc(s_T, 1)
                n_["T"] += 32

            nc.all_engine_barrier()
            if STOP == "p3b":
                _finish(nc, x_tok, out_hbm, sd_x, n_)
                return nc

        with nc.Block() as blk:
            V0, S0 = n_["V"], n_["S"]

            @blk.vector
            def _(v):
                for t in range(T):
                    xt = x_tok.ap()[:, t * D:(t + 1) * D]
                    v.drain()
                    v.bn_stats(bnst[:, 0:6], xt[:, 0:512])
                    v.bn_stats(bnst[:, 6:12], xt[:, 512:1024])
                    v.tensor_reduce(xmax_a.ap()[:, t:t + 1], xt, AX, OP.max)
                    v.tensor_reduce(xmin_a.ap()[:, t:t + 1], xt, AX, OP.min,
                                    negate=False)
                    v.drain()
                    v.bn_aggr(muvar.ap()[:, 2 * t:2 * t + 2], bnst[:, :]
                              ).then_inc(s_V, 1)
                mu_v = muvar.ap().rearrange("p (t k) -> p t k", k=2)[:, :, 0:1]
                v.wait_ge(s_S, S0 + 1)
                v.drain()
                v.reciprocal(rstd_all[:, 0:T], sqv[:, :])
                v.tensor_sub(xmax_a[:, :], xmax_a[:, :], mu_v)
                v.tensor_sub(dev_b[:, :], mu_v, xmin_a[:, :])
                v.drain()
                v.tensor_max(xmax_a[:, :], xmax_a[:, :], dev_b[:, :])
                v.drain()
                v.tensor_mul(xmax_a[:, :], xmax_a[:, :], rstd_all[:, 0:T])
                v.drain()
                v.drain()
                v.tensor_reduce(amax_run[:, :], xmax_a[:, :], AX, OP.max
                                ).then_inc(s_V, 1)
            n_["V"] += T + 1

            @blk.scalar
            def _(s):
                s.wait_ge(s_V, V0 + T)
                var_v = muvar.ap().rearrange("p (t k) -> p t k", k=2)[:, :, 1:2]
                s.activation(sqv[:, :], var_v, AF.Sqrt,
                             bias=eps_r[:, :], scale=1.0).then_inc(s_S, 1)
            n_["S"] += 1

        nc.all_engine_barrier()

        with ExitStack() as fctx:
            fn = fctx.enter_context
            hpreT = fn(nc.sbuf_tensor("hpreT", [128, IC * SEQ], BF16))
            cc_stage = fn(nc.sbuf_tensor("cc_stage", [1, 1], F32))
            amax2 = fn(nc.sbuf_tensor("amax2", [128, 1], F32))
            scrA = fn(nc.sbuf_tensor("scrA", [128, D], F32))
            xstage = fn(nc.sbuf_tensor("xstage", [128, 2 * D], BF16))
            scrB = fn(nc.sbuf_tensor("scrB", [128, 2 * D], F32))

            if STOP == "exch1":
                _finish(nc, x_tok, out_hbm, sd_x, n_)
                return nc

            with ExitStack() as f2:
                f2n = f2.enter_context
                xqT = f2n(nc.sbuf_tensor("xqT", [128, C * SEQ], BF16))
                wgq = f2n(nc.sbuf_tensor("wgq", [128, D], BF16))
                wuq = f2n(nc.sbuf_tensor("wuq", [128, D], BF16))
                wgT = f2n(nc.sbuf_tensor("wgT", [128, 2 * D], BF16))
                wuT = f2n(nc.sbuf_tensor("wuT", [128, 2 * D], BF16))
                silu_sb = f2n(nc.sbuf_tensor("silu_sb", [128, 1536], BF16))

                with (
                    nc.psum_tensor("p_sm4b", [1, 128], F32) as p_sm4b,
                    nc.Block() as blk,
                ):
                    V0, S0, T0 = n_["V"], n_["S"], n_["T"]

                    @blk.tensor
                    def _(tensor):
                        tensor.transpose(p_sm4b[0:1, :], amax_run[:, 0:1], idf[:, :]
                                         ).then_inc(s_T, 1)
                    n_["T"] += 1

                    @blk.vector
                    def _(v):
                        v.memset(cc_stage[:, :], 0.0)
                        v.drain()
                        v.wait_ge(s_T, T0 + 1)
                        v.tensor_reduce(cc_stage.ap()[0:1, 0:1], p_sm4b[0:1, :],
                                        AX, OP.max).then_inc(s_V, 1)
                    n_["V"] += 1

                    @blk.scalar
                    def _(s):
                        s.wait_ge(s_V, V0 + 1)
                        s.dma_start(out=cc_in1.ap()[0:1, 0:1],
                                    in_=cc_stage[:, :]).then_inc(sd_w, 16)
                        s.activation(tmp_b[:, :], clipx_r[:, :], AF.Sigmoid
                                     ).then_inc(s_S, 1)
                        s.wait_ge(s_cc, 1)
                        s.dma_start(out=gmax_r[:, :],
                                    in_=cc_out1.ap()[0:1, 0:1].to_broadcast([128, 1])
                                    ).then_inc(sd_w, 16)
                    n_["S"] += 1

                    @blk.gpsimd
                    def _(g):
                        g.wait_ge(sd_w, n_["dw"] + 16)
                        g.collective_compute(
                            "AllReduce", OP.max,
                            replica_groups=[list(range(NCORES))],
                            ins=[cc_in1.ap().opt()], outs=[cc_out1.ap().opt()],
                            ).then_inc(s_cc, 1)
                    n_["dw"] += 32
                    n_["cc"] += 1

                    @blk.vector
                    def _(v):
                        v.wait_ge(sd_w, n_["dw"])
                        v.wait_ge(s_S, S0 + 1)
                        v.tensor_mul(sc_sx[:, :], tmp_b[:, :], gmax_r[:, :])
                        v.drain()
                        v.tensor_scalar(sc_sx[:, :], sc_sx[:, :], 1.0 / 127.0,
                                        None, OP.mult)
                        v.drain()
                        v.reciprocal(sc_rx[:, :], sc_sx[:, :])
                        v.drain()
                        v.tensor_scalar(amat.ap()[:, 0:T], rstd_all[:, 0:T],
                                        sc_rx.ap()[:, 0:1], None, OP.mult)
                        v.drain()
                        mu_v = muvar.ap().rearrange("p (t k) -> p t k",
                                                    k=2)[:, :, 0:1]
                        v.tensor_mul(bmat.ap()[:, 0:T], mu_v, amat.ap()[:, 0:T])
                        v.drain()
                        v.tensor_scalar(bmat.ap()[:, 0:T], bmat.ap()[:, 0:T],
                                        -1.0, None, OP.mult).then_inc(s_V, 1)
                    n_["V"] += 1

                nc.all_engine_barrier()

                with nc.Block() as blk:
                    V0, T0, S0 = n_["V"], n_["T"], n_["S"]
                    pos_q = {t: V0 + t + 1 for t in range(T)}
                    tq_after = {}
                    tqc = [0, 0]
                    for t in range(T):
                        tqc[t % 2] += 16
                        tq_after[t] = tqc[t % 2]

                    @blk.scalar
                    def _(s):
                        for t in range(T):
                            xt = x_tok.ap()[:, t * D:(t + 1) * D]
                            s.drain()
                            s.activation(scrA[:, :], xt, AF.Identity,
                                         scale=amat.ap()[:, t:t + 1],
                                         bias=bmat.ap()[:, t:t + 1])
                            if t >= 2:
                                s.wait_ge(s_V, pos_q[t - 2])
                            s.drain()
                            s.activation(scrB[:, (t % 2) * D:(t % 2 + 1) * D],
                                         scrA[:, :], AF.Identity,
                                         bias=magic_r[:, :], scale=1.0
                                         ).then_inc(s_S, 1)
                    n_["S"] += T

                    @blk.vector
                    def _(v):
                        for t in range(T):
                            wq = wgq if t % 2 == 0 else wuq
                            if t >= 2:
                                v.wait_ge(sd_tq[t % 2], tq_after[t - 2])
                            v.wait_ge(s_S, S0 + t + 1)
                            v.drain()
                            v.tensor_scalar(scr1[:, :],
                                            scrB[:, (t % 2) * D:(t % 2 + 1) * D],
                                            MAGIC, 127.0,
                                            OP.subtract, OP.min)
                            v.drain()
                            v.tensor_scalar(wq[:, :], scr1[:, :],
                                            -128.0, None, OP.max
                                            ).then_inc(s_V, 1)
                    n_["V"] += T

                    @blk.sync
                    def _(sync):
                        for t in range(T):
                            wq = wgq if t % 2 == 0 else wuq
                            sync.wait_ge(s_V, pos_q[t])
                            if t >= 2:
                                sync.wait_ge(sd_tx[t % 2], 16 * (t // 2))
                            sync.dma_start_transpose(
                                xstage.ap()[:, (t % 2) * D:(t % 2 + 1) * D
                                            ].rearrange("p (j c) -> p j c",
                                                        c=128),
                                wq.ap().rearrange("p (j c) -> p j c", c=128)
                                ).then_inc(sd_tq[t % 2], 16)
                            sync.wait_ge(sd_tq[t % 2], 16 * (t // 2 + 1))
                            sync.dma_start(
                                out=xqT.ap().rearrange(
                                    "p (j s) -> p j s",
                                    s=SEQ)[:, :, t * 128:(t + 1) * 128],
                                in_=xstage.ap()[:, (t % 2) * D:(t % 2 + 1) * D
                                                ].rearrange("p (j c) -> p j c",
                                                            c=128)
                                ).then_inc(sd_tx[t % 2], 16)
                        sync.wait_ge(sd_tx[0], 16 * (T // 2))
                        sync.wait_ge(sd_tx[1], 16 * (T // 2))

                nc.all_engine_barrier()

                T0, S0, V0 = n_["T"], n_["S"], n_["V"]
                dma_g = {mi: 16 * (mi // 2 + 1) for mi in range(IC)}
                dma_u = {mi: 16 * (mi // 2 + 1) for mi in range(IC)}
                pos_stt = {(mi, q4): V0 + (mi * 4 + q4) * 2 + 1
                           for mi in range(IC) for q4 in range(4)}
                pos_macc = {(mi, q4): V0 + (mi * 4 + q4) * 2 + 2
                            for mi in range(IC) for q4 in range(4)}
                vcn = V0 + IC * 8
                pos_gmm, pos_umm = {}, {}
                tcn = T0
                for mi_ in range(IC):
                    for q4 in range(4):
                        tcn += 1
                        pos_gmm[mi_, q4] = tcn
                        tcn += 1
                        pos_umm[mi_, q4] = tcn
                pos_silu = {}
                scn = S0
                for mi_ in range(IC):
                    for q4 in range(4):
                        scn += 1
                        pos_silu[mi_, q4] = scn

                with (
                    nc.psum_tensor("p_g", [128, 1536], F32) as p_g,
                    nc.psum_tensor("p_u", [128, 1536], F32) as p_u,
                    nc.Block() as blk,
                ):
                    @blk.scalar
                    def _(s):
                        s.wait_ge(sd_wb[0], n_["wb0"])
                        s.wait_ge(sd_wb[1], n_["wb1"])

                        def dma_pair(mi):
                            if mi >= 2:
                                s.wait_ge(s_T, pos_umm[mi - 2, 3])
                            s.dma_start(
                                out=wgT.ap()[:, (mi % 2) * D:(mi % 2 + 1) * D],
                                in_=wgT_hbm.ap()[:, mi * D:(mi + 1) * D]
                                ).then_inc(sd_wg[mi % 2], 16)
                            s.dma_start(
                                out=wuT.ap()[:, (mi % 2) * D:(mi % 2 + 1) * D],
                                in_=wuT_hbm.ap()[:, mi * D:(mi + 1) * D]
                                ).then_inc(sd_wu[mi % 2], 16)
                        dma_pair(0)
                        dma_pair(1)
                        for mi in range(IC):
                            for q4 in range(4):
                                gq = mi * 4 + q4
                                sl = slice((gq % 3) * 512, (gq % 3) * 512 + 512)
                                if gq >= 3:
                                    m2, q2 = divmod(gq - 3, 4)
                                    s.wait_ge(s_V, pos_stt[m2, q2])
                                s.wait_ge(s_T, pos_gmm[mi, q4])
                                s.activation(silu_sb[:, sl], p_g[:, sl],
                                             AF.Silu, scale=sc_sx.ap()[:, 0:1]
                                             ).then_inc(s_S, 1)
                            if mi + 2 < IC:
                                dma_pair(mi + 2)
                    n_["S"] = scn

                    @blk.vector
                    def _(v):
                        def epi(m):
                            for q4 in range(4):
                                gq = m * 4 + q4
                                sl = slice((gq % 3) * 512, (gq % 3) * 512 + 512)
                                v.wait_ge(s_S, pos_silu[m, q4])
                                v.wait_ge(s_T, pos_umm[m, q4])
                                hsl = slice(m * SEQ + q4 * 512,
                                            m * SEQ + (q4 + 1) * 512)
                                v.scalar_tensor_tensor(
                                    hpreT.ap()[:, hsl], p_u[:, sl],
                                    sc_sx.ap()[:, 0:1], silu_sb[:, sl],
                                    OP.mult, OP.mult).then_inc(s_V, 1)
                                v.drain()
                                v.tensor_reduce(tmp_e[:, :], hpreT.ap()[:, hsl],
                                                AX, OP.max,
                                                apply_absolute_value=True)
                                v.drain()
                                if m == 0 and q4 == 0:
                                    v.tensor_copy(amax2[:, :], tmp_e[:, :]
                                                  ).then_inc(s_V, 1)
                                else:
                                    v.tensor_max(amax2[:, :], amax2[:, :],
                                                 tmp_e[:, :]).then_inc(s_V, 1)
                        for mi in range(IC):
                            epi(mi)
                    n_["V"] = vcn

                    @blk.tensor
                    def _(tensor):
                        for mi in range(IC):
                            tensor.wait_ge(sd_wg[mi % 2], dma_g[mi])
                            tensor.wait_ge(sd_wu[mi % 2], dma_u[mi])
                            for q4 in range(4):
                                gq = mi * 4 + q4
                                sl = slice((gq % 3) * 512, (gq % 3) * 512 + 512)
                                if gq >= 3:
                                    m2, q2 = divmod(gq - 3, 4)
                                    tensor.wait_ge(s_S, pos_silu[m2, q2])
                                for kc in range(C):
                                    mm = tensor.matmul(
                                        p_g[:, sl],
                                        wgT.ap()[:, (mi % 2) * D + kc * 128:
                                                 (mi % 2) * D + (kc + 1) * 128],
                                        xqT.ap()[:, kc * SEQ + q4 * 512:
                                                 kc * SEQ + (q4 + 1) * 512],
                                        start=(kc == 0), stop=(kc == C - 1))
                                mm.then_inc(s_T, 1)
                                if gq >= 3:
                                    m2, q2 = divmod(gq - 3, 4)
                                    tensor.wait_ge(s_V, pos_stt[m2, q2])
                                for kc in range(C):
                                    mm = tensor.matmul(
                                        p_u[:, sl],
                                        wuT.ap()[:, (mi % 2) * D + kc * 128:
                                                 (mi % 2) * D + (kc + 1) * 128],
                                        xqT.ap()[:, kc * SEQ + q4 * 512:
                                                 kc * SEQ + (q4 + 1) * 512],
                                        start=(kc == 0), stop=(kc == C - 1))
                                mm.then_inc(s_T, 1)
                    n_["T"] = tcn

                nc.all_engine_barrier()
                if STOP == "p5":
                    _finish(nc, x_tok, out_hbm, sd_x, n_)
                    return nc

                nc.all_engine_barrier()


            with nc.sbuf_tensor("wdT", [128, IC * D], BF16) as wdT:
                with (
                    nc.psum_tensor("p_sm5", [1, 128], F32) as p_sm5,
                    nc.Block() as blk,
                ):
                    V0, S0, T0 = n_["V"], n_["S"], n_["T"]

                    @blk.tensor
                    def _(tensor):
                        tensor.transpose(p_sm5[0:1, :], amax2[:, 0:1], idf[:, :]
                                         ).then_inc(s_T, 1)
                    n_["T"] += 1

                    @blk.vector
                    def _(v):
                        v.memset(cc_stage[:, :], 0.0)
                        v.drain()
                        v.wait_ge(s_T, T0 + 1)
                        v.tensor_reduce(cc_stage.ap()[0:1, 0:1], p_sm5[0:1, :],
                                        AX, OP.max).then_inc(s_V, 1)
                    n_["V"] += 1

                    @blk.scalar
                    def _(s):
                        s.wait_ge(s_V, V0 + 1)
                        s.dma_start(out=cc_in2.ap()[0:1, 0:1],
                                    in_=cc_stage[:, :]).then_inc(sd_w, 16)
                        s.activation(tmp_b[:, :], clipd_r[:, :], AF.Sigmoid
                                     ).then_inc(s_S, 1)
                        for ic in range(IC):
                            s.dma_start(out=wdT.ap()[:, ic * D:(ic + 1) * D],
                                        in_=wdT_hbm.ap()[:, ic * D:(ic + 1) * D]
                                        ).then_inc(sd_wd, 16)
                        s.wait_ge(s_cc, 2)
                        s.dma_start(out=gmax_r[:, :],
                                    in_=cc_out2.ap()[0:1, 0:1].to_broadcast(
                                        [128, 1])).then_inc(sd_w, 16)
                    n_["S"] += 1

                    @blk.gpsimd
                    def _(g):
                        g.wait_ge(sd_w, n_["dw"] + 16)
                        g.collective_compute(
                            "AllReduce", OP.max,
                            replica_groups=[list(range(NCORES))],
                            ins=[cc_in2.ap().opt()], outs=[cc_out2.ap().opt()],
                            ).then_inc(s_cc, 1)
                    n_["dw"] += 32
                    n_["cc"] += 1

                    @blk.vector
                    def _(v):
                        v.wait_ge(sd_w, n_["dw"])
                        v.wait_ge(s_S, S0 + 1)
                        v.tensor_mul(sc_sh[:, :], tmp_b[:, :], gmax_r[:, :])
                        v.drain()
                        v.tensor_scalar(sc_sh[:, :], sc_sh[:, :], 1.0 / 127.0,
                                        None, OP.mult)
                        v.drain()
                        v.reciprocal(sc_rh[:, :], sc_sh[:, :]).then_inc(s_V, 1)
                    n_["V"] += 1


                nc.all_engine_barrier()
                with (
                    nc.psum_tensor("p_d", [128, 1536], F32) as p_d,
                    nc.Block() as blk,
                ):
                    T0, V0, S0, P0_ = n_["T"], n_["V"], n_["S"], n_["P"]
                    pos_c1 = {k_: P0_ + 2 * k_ + 1 for k_ in range(2 * IC)}
                    pos_qh = {(hh, ic): P0_ + 2 * (hh * IC + ic) + 2
                              for hh in range(2) for ic in range(IC)}
                    pos_add = {idx: V0 + idx + 1 for idx in range(32)}
                    pos_act = {k_: S0 + k_ + 1 for k_ in range(2 * IC)}
                    pos_dmm = {idx: T0 + idx + 1 for idx in range(32)}

                    @blk.scalar
                    def _(s):
                        for hh in range(2):
                            for ic in range(IC):
                                k_ = hh * IC + ic
                                sl = slice(ic * SEQ + hh * 1024,
                                           ic * SEQ + (hh + 1) * 1024)
                                if k_ >= 2:
                                    s.wait_ge(s_P, pos_c1[k_ - 2])
                                s.drain()
                                s.activation(scrB[:, (k_ % 2) * D:(k_ % 2 + 1) * D],
                                             hpreT.ap()[:, sl],
                                             AF.Identity,
                                             scale=sc_rh.ap()[:, 0:1],
                                             bias=magic_r[:, :]).then_inc(s_S, 1)
                    n_["S"] += 2 * IC

                    @blk.gpsimd
                    def _(g):
                        for hh in range(2):
                            for ic in range(IC):
                                k_ = hh * IC + ic
                                sl = slice(ic * SEQ + hh * 1024,
                                           ic * SEQ + (hh + 1) * 1024)
                                g.wait_ge(s_S, pos_act[k_])
                                g.drain()
                                g.tensor_scalar(scr1[:, :],
                                                scrB[:, (k_ % 2) * D:(k_ % 2 + 1) * D],
                                                MAGIC, 127.0,
                                                OP.subtract, OP.min
                                                ).then_inc(s_P, 1)
                                g.drain()
                                g.tensor_scalar(hpreT.ap()[:, sl], scr1[:, :],
                                                -128.0, None, OP.max
                                                ).then_inc(s_P, 1)
                    n_["P"] = P0_ + 4 * IC

                    @blk.vector
                    def _(v):
                        for idx in range(32):
                            t, piece = divmod(idx, 2)
                            sl = slice((idx % 3) * 512, (idx % 3) * 512 + 512)
                            xsl = slice(t * D + piece * 512,
                                        t * D + piece * 512 + 512)
                            v.wait_ge(s_T, pos_dmm[idx])
                            v.scalar_tensor_tensor(
                                x_tok.ap()[:, xsl], p_d[:, sl],
                                sc_sh.ap()[:, 0:1], x_tok.ap()[:, xsl],
                                OP.mult, OP.add).then_inc(s_V, 1)
                    n_["V"] = V0 + 32

                    @blk.tensor
                    def _(tensor):
                        tensor.wait_ge(sd_wd, 16 * IC)
                        for idx in range(32):
                            t, piece = divmod(idx, 2)
                            hh = t // 8
                            sl = slice((idx % 3) * 512, (idx % 3) * 512 + 512)
                            if idx == hh * 16:
                                tensor.wait_ge(s_P, pos_qh[hh, IC - 1])
                            if idx >= 3:
                                tensor.wait_ge(s_V, pos_add[idx - 3])
                            for ic in range(IC):
                                mm = tensor.matmul(
                                    p_d[:, sl],
                                    hpreT.ap()[:, ic * SEQ + t * 128:
                                               ic * SEQ + (t + 1) * 128],
                                    wdT.ap()[:, ic * D + piece * 512:
                                             ic * D + (piece + 1) * 512],
                                    start=(ic == 0), stop=(ic == IC - 1))
                            mm.then_inc(s_T, 1)
                    n_["T"] = T0 + 32

                    @blk.sync
                    def _(sync):
                        for t in range(T):
                            sync.wait_ge(s_V, pos_add[2 * t + 1])
                            sync.dma_start(
                                out=out_hbm[t * 128:(t + 1) * 128, :],
                                in_=x_tok.ap()[:, t * D:(t + 1) * D]
                                ).then_inc(sd_x, 16)
                        sync.wait_ge(sd_x, n_["dx"] + 16 * T)

    return nc


def kernel(L0_attn_x, L0_attn_w_qkv, L0_attn_w_out, L0_ffn_clip_x,
           L0_ffn_clip_down, L0_ffn_w_gate, L0_ffn_w_up, L0_ffn_w_down):
    x = np.ascontiguousarray(np.asarray(L0_attn_x, dtype=np.float32))
    nc = build_nc()
    common = dict(
        w_qkv=np.ascontiguousarray(np.asarray(L0_attn_w_qkv, np.float32)),
        w_out=np.ascontiguousarray(np.asarray(L0_attn_w_out, np.float32)),
        clip_x=np.ascontiguousarray(np.asarray(L0_ffn_clip_x, np.float32)),
        clip_d=np.ascontiguousarray(np.asarray(L0_ffn_clip_down, np.float32)),
        w_gate=np.ascontiguousarray(np.asarray(L0_ffn_w_gate, np.float32)),
        w_up=np.ascontiguousarray(np.asarray(L0_ffn_w_up, np.float32)),
        w_down=np.ascontiguousarray(np.asarray(L0_ffn_w_down, np.float32)),
    )
    in_maps = [dict(common, x=np.ascontiguousarray(x[b])) for b in range(NCORES)]
    res = run_bass_kernel_spmd(nc, in_maps, core_ids=list(range(NCORES)))
    return np.stack([res.results[b]["out"] for b in range(NCORES)], axis=0)


# version 24
# speedup vs baseline: 1.4869x; 1.4869x over previous
import numpy as np
import ml_dtypes
import concourse.bass as bass
import concourse.mybir as mybir
from concourse.bass_utils import run_bass_kernel_spmd

F32 = mybir.dt.float32
BF16 = mybir.dt.bfloat16
OP = mybir.AluOpType
AX = mybir.AxisListType.X
AF = mybir.ActivationFunctionType

SEQ = 2048
D = 1024
NH = 64
II = 2816
T = SEQ // 128
C = D // 128
IC = II // 128
MAGIC = 12582912.0
EPS = 1e-5
HALF = 1024
NCORES = 8
NHEADS = 2 * NH
STOP = "all"


def _finish(nc, x_tok, out_hbm, sd_x, n_):
    with nc.Block() as blk:
        @blk.sync
        def _(sync):
            for t in range(T):
                sync.dma_start(out=out_hbm[t * 128:(t + 1) * 128, :],
                               in_=x_tok.ap()[:, t * D:(t + 1) * D]
                               ).then_inc(sd_x, 16)
            sync.wait_ge(sd_x, n_["dx"] + 16 * T)


def _attn_heads():
    heads = []
    for ih in range(2):
        for h in range(NH):
            chunks = []
            for jc in range(8 if ih == 0 else 16):
                i_lo = max(jc * 128 - ih * HALF, 0)
                diag = (jc * 128) >= ih * HALF
                chunks.append((jc, i_lo, diag))
            heads.append((ih * NH + h, ih, h, chunks))
    return heads


def _nd_pieces(i_lo):
    return [(p, max(i_lo, 512 * p), 512 * (p + 1))
            for p in range(2) if max(i_lo, 512 * p) < 512 * (p + 1)]


def build_nc():
    nc = bass.Bass("TRN2", target_bir_lowering=False,
                   dynamic_dma_scratch_size=512)

    x_hbm = nc.dram_tensor("x", [SEQ, D], F32, kind="ExternalInput").ap()
    wqkv_hbm = nc.dram_tensor("w_qkv", [192, D], F32, kind="ExternalInput").ap()
    wout_hbm = nc.dram_tensor("w_out", [D, NH], F32, kind="ExternalInput").ap()
    clipx_hbm = nc.dram_tensor("clip_x", [1], F32, kind="ExternalInput").ap()
    clipd_hbm = nc.dram_tensor("clip_d", [1], F32, kind="ExternalInput").ap()
    wg_hbm = nc.dram_tensor("w_gate", [II, D], F32, kind="ExternalInput").ap()
    wu_hbm = nc.dram_tensor("w_up", [II, D], F32, kind="ExternalInput").ap()
    wd_hbm = nc.dram_tensor("w_down", [D, II], F32, kind="ExternalInput").ap()
    out_hbm = nc.dram_tensor("out", [SEQ, D], F32, kind="ExternalOutput").ap()

    cc_in1 = nc.dram_tensor("cc_in1", [1, 2048], F32)
    cc_out1 = nc.dram_tensor("cc_out1", [1, 2048], F32)
    cc_in2 = nc.dram_tensor("cc_in2", [1, 2048], F32)
    cc_out2 = nc.dram_tensor("cc_out2", [1, 2048], F32)
    wgT_hbm = nc.dram_tensor("wgT_hbm", [128, IC * D], BF16)
    wuT_hbm = nc.dram_tensor("wuT_hbm", [128, IC * D], BF16)
    wdT_hbm = nc.dram_tensor("wdT_hbm", [128, IC * D], BF16)

    tri_c = nc.inline_tensor(
        (np.arange(128)[None, :] >= np.arange(128)[:, None]
         ).astype(ml_dtypes.bfloat16), name="tri_c")
    idf_c = nc.inline_tensor(np.eye(128, dtype=np.float32), name="idf_c")
    idb_c = nc.inline_tensor(np.eye(128).astype(ml_dtypes.bfloat16), name="idb_c")
    sel_c = nc.inline_tensor(
        np.kron(np.eye(64), np.ones((1, 128))).astype(ml_dtypes.bfloat16),
        name="sel_c")
    ones_c = nc.inline_tensor(np.ones((1, 128), dtype=np.float32), name="ones_c")
    zero_c = nc.inline_tensor(np.zeros((1, 2048), dtype=np.float32),
                              name="zero_c")

    from contextlib import ExitStack
    with ExitStack() as ctx:
        en = ctx.enter_context

        x_tok = en(nc.sbuf_tensor("x_tok", [128, T * D], F32))
        tri = en(nc.sbuf_tensor("tri", [128, 128], BF16))
        idf = en(nc.sbuf_tensor("idf", [128, 128], F32))
        idb = en(nc.sbuf_tensor("idb", [128, 128], BF16))
        ones1 = en(nc.sbuf_tensor("ones1", [1, 128], F32))
        clipx_r = en(nc.sbuf_tensor("clipx_r", [128, 1], F32))
        clipd_r = en(nc.sbuf_tensor("clipd_r", [128, 1], F32))
        mu_all = en(nc.sbuf_tensor("mu_all", [128, T], F32))
        rstd_all = en(nc.sbuf_tensor("rstd_all", [128, T], F32))
        amax_run = en(nc.sbuf_tensor("amax_run", [128, 1], F32))
        sc_sx = en(nc.sbuf_tensor("sc_sx", [128, 1], F32))
        sc_rx = en(nc.sbuf_tensor("sc_rx", [128, 1], F32))
        sc_sh = en(nc.sbuf_tensor("sc_sh", [128, 1], F32))
        sc_rh = en(nc.sbuf_tensor("sc_rh", [128, 1], F32))
        gmax_r = en(nc.sbuf_tensor("gmax_r", [128, 1], F32))
        tmp_a = en(nc.sbuf_tensor("tmp_a", [128, 1], F32))
        tmp_b = en(nc.sbuf_tensor("tmp_b", [128, 1], F32))
        tmp_c = en(nc.sbuf_tensor("tmp_c", [128, 1], F32))
        tmp_d = en(nc.sbuf_tensor("tmp_d", [128, 1], F32))
        tmp_e = en(nc.sbuf_tensor("tmp_e", [128, 1], F32))
        tmp_f = en(nc.sbuf_tensor("tmp_f", [128, 1], F32))
        amat = en(nc.sbuf_tensor("amat", [128, T], F32))
        bmat = en(nc.sbuf_tensor("bmat", [128, T], F32))
        scr1 = en(nc.sbuf_tensor("scr1", [128, 1024], F32))
        bnst = en(nc.sbuf_tensor("bnst", [128, 12], F32))
        muvar = en(nc.sbuf_tensor("muvar", [128, 2 * T], F32))
        xmax_a = en(nc.sbuf_tensor("xmax_a", [128, T], F32))
        xmin_a = en(nc.sbuf_tensor("xmin_a", [128, T], F32))
        dev_b = en(nc.sbuf_tensor("dev_b", [128, T], F32))
        sqv = en(nc.sbuf_tensor("sqv", [128, T], F32))
        magic_r = en(nc.sbuf_tensor("magic_r", [128, 1], F32))
        eps_r = en(nc.sbuf_tensor("eps_r", [128, 1], F32))

        sd_x = en(nc.semaphore("sd_x"))
        sd_xa = tuple(en(nc.semaphore("sd_xa%d" % i)) for i in range(4))
        sd_xb = tuple(en(nc.semaphore("sd_xb%d" % i)) for i in range(4))
        sd_w = en(nc.semaphore("sd_w"))
        s_T = en(nc.semaphore("s_T"))
        s_S = en(nc.semaphore("s_S"))
        s_V = en(nc.semaphore("s_V"))
        s_cc = en(nc.semaphore("s_cc"))
        sd_p0 = en(nc.semaphore("sd_p0"))
        sd_p1 = en(nc.semaphore("sd_p1"))
        sd_wg0 = en(nc.semaphore("sd_wg0"))
        sd_wg1 = en(nc.semaphore("sd_wg1"))
        sd_wu0 = en(nc.semaphore("sd_wu0"))
        sd_wu1 = en(nc.semaphore("sd_wu1"))
        sd_wg = (sd_wg0, sd_wg1)
        sd_wu = (sd_wu0, sd_wu1)
        sd_tg = (en(nc.semaphore("sd_tg0")), en(nc.semaphore("sd_tg1")))
        sd_tu = (en(nc.semaphore("sd_tu0")), en(nc.semaphore("sd_tu1")))
        sd_tq = (en(nc.semaphore("sd_tq0")), en(nc.semaphore("sd_tq1")))
        sd_td = (en(nc.semaphore("sd_td0")), en(nc.semaphore("sd_td1")))
        sd_tl = (en(nc.semaphore("sd_tl0")), en(nc.semaphore("sd_tl1")))
        sd_ts = (en(nc.semaphore("sd_ts0")), en(nc.semaphore("sd_ts1")))
        sd_tx = (en(nc.semaphore("sd_tx0")), en(nc.semaphore("sd_tx1")))
        sd_wd = en(nc.semaphore("sd_wd"))
        sd_wr = (en(nc.semaphore("sd_wr0")), en(nc.semaphore("sd_wr1")))
        sd_wt = (en(nc.semaphore("sd_wt0")), en(nc.semaphore("sd_wt1")))
        sd_wb = (en(nc.semaphore("sd_wb0")), en(nc.semaphore("sd_wb1")))
        s_wq = en(nc.semaphore("s_wq"))
        n_ = {"dx": 0, "dw": 0, "T": 0, "S": 0, "V": 0, "cc": 0}

        with ExitStack() as actx:
            an = actx.enter_context
            sel = an(nc.sbuf_tensor("sel", [64, NH * 128], BF16))
            wqkvT = an(nc.sbuf_tensor("wqkvT", [128, C * 192], BF16))
            woutT = an(nc.sbuf_tensor("woutT", [64, D], BF16))
            qb = an(nc.sbuf_tensor("qb", [64, SEQ], BF16))
            kT = an(nc.sbuf_tensor("kT", [128, T * NH], F32))
            vT_p = an(nc.sbuf_tensor("vT_p", [128, T * 128], BF16))
            negm = an(nc.sbuf_tensor("negm", [128, NH], F32))
            negm_row = an(nc.sbuf_tensor("negm_row", [1, NH], F32))
            E_buf = an(nc.sbuf_tensor("E_buf", [128, 6 * HALF], BF16))
            pair_sb = an(nc.sbuf_tensor("pair_sb", [2, 2 * HALF], F32))
            na_sb = an(nc.sbuf_tensor("na_sb", [64, SEQ], F32))
            da_sb = an(nc.sbuf_tensor("da_sb", [64, SEQ], F32))
            attn_o = qb
            ln_tile = an(nc.sbuf_tensor("ln_tile", [128, 2 * D], BF16))
            ln1T = an(nc.sbuf_tensor("ln1T", [128, C * SEQ], BF16))

            with (
                nc.sbuf_tensor("wq_a", [128, D], F32) as wq_a,
                nc.sbuf_tensor("wq_b", [64, D], F32) as wq_b,
                nc.sbuf_tensor("wo_raw", [128, C * NH], F32) as wo_raw,
                nc.psum_tensor("p_t0", [128, 2048], F32) as p_t0,
                nc.Block() as blk,
            ):
                @blk.sync
                def _(sync):
                    for t in range(0, T, 2):
                        i2 = t // 2
                        if i2 >= 4:
                            sync.wait_ge(sd_xa[i2 % 4], 16 * (i2 // 4))
                        sync.dma_start(out=x_tok.ap()[:, t * D:(t + 1) * D],
                                       in_=x_hbm[t * 128:(t + 1) * 128, :]
                                       ).then_inc(sd_xa[i2 % 4], 16)

                @blk.scalar
                def _(g):
                    for dst, src in ((tri, tri_c), (idf, idf_c), (idb, idb_c),
                                     (ones1, ones_c), (sel, sel_c)):
                        g.dma_start(out=dst[:, :], in_=src.ap()).then_inc(sd_w, 16)
                    g.dma_start(out=clipx_r[:, :],
                                in_=clipx_hbm[None, :].to_broadcast([128, 1])
                                ).then_inc(sd_w, 16)
                    g.dma_start(out=clipd_r[:, :],
                                in_=clipd_hbm[None, :].to_broadcast([128, 1])
                                ).then_inc(sd_w, 16)
                    g.dma_start(out=wq_a[:, :], in_=wqkv_hbm[0:128, :]
                                ).then_inc(sd_w, 16)
                    g.dma_start(out=wq_b[:, :], in_=wqkv_hbm[128:192, :]
                                ).then_inc(sd_w, 16)
                    g.dma_start(out=wo_raw.ap().rearrange("p (c h) -> p c h", c=C),
                                in_=wout_hbm.rearrange("(c p) h -> p c h", p=128)
                                ).then_inc(sd_w, 16)
                    g.dma_start(out=cc_in1.ap(), in_=zero_c.ap()
                                ).then_inc(sd_w, 16)
                    g.dma_start(out=cc_in2.ap(), in_=zero_c.ap()
                                ).then_inc(sd_w, 16)
                    for t in range(1, T, 2):
                        i2 = t // 2
                        if i2 >= 4:
                            g.wait_ge(sd_xb[i2 % 4], 16 * (i2 // 4))
                        g.dma_start(out=x_tok.ap()[:, t * D:(t + 1) * D],
                                    in_=x_hbm[t * 128:(t + 1) * 128, :]
                                    ).then_inc(sd_xb[i2 % 4], 16)
                n_["dw"] += 12 * 16

                items = ([("qk", kc) for kc in range(C)]
                         + [("v", kc) for kc in range(C)]
                         + [("wo", kc) for kc in range(C)])
                if STOP == "load":
                    items = []
                T0, V0 = n_["T"], n_["V"]

                @blk.tensor
                def _(tensor):
                    tensor.wait_ge(sd_w, n_["dw"])
                    for i, (kind, kc) in enumerate(items):
                        slot = (i % 4) * 512
                        if i >= 4:
                            tensor.wait_ge(s_V, V0 + (i - 4) + 1)
                        if kind == "qk":
                            tensor.transpose(p_t0[:, slot:slot + 128],
                                             wq_a[:, kc * 128:(kc + 1) * 128],
                                             idf[:, :]).then_inc(s_T, 1)
                        elif kind == "v":
                            tensor.transpose(p_t0[:, slot:slot + 64],
                                             wq_b[:, kc * 128:(kc + 1) * 128],
                                             idf[0:64, 0:64]).then_inc(s_T, 1)
                        else:
                            tensor.transpose(p_t0[0:64, slot:slot + 128],
                                             wo_raw[:, kc * NH:(kc + 1) * NH],
                                             idf[:, :]).then_inc(s_T, 1)
                n_["T"] += len(items)

                @blk.vector
                def _(v):
                    for i, (kind, kc) in enumerate(items):
                        slot = (i % 4) * 512
                        v.wait_ge(s_T, T0 + i + 1)
                        if kind == "qk":
                            v.tensor_copy(wqkvT.ap()[:, kc * 192:kc * 192 + 128],
                                          p_t0[:, slot:slot + 128]).then_inc(s_V, 1)
                        elif kind == "v":
                            v.tensor_copy(
                                wqkvT.ap()[:, kc * 192 + 128:kc * 192 + 192],
                                p_t0[:, slot:slot + 64]).then_inc(s_V, 1)
                        else:
                            v.tensor_copy(woutT.ap()[:, kc * 128:(kc + 1) * 128],
                                          p_t0[0:64, slot:slot + 128]).then_inc(s_V, 1)
                    v.memset(vT_p[:, :], 1.0)
                    v.memset(eps_r[:, :], EPS)
                    v.memset(magic_r[:, :], MAGIC).then_inc(s_V, 1)
                n_["V"] += len(items) + 1

            v_after_p0 = n_["V"]
            if STOP in ("p0", "load"):
                _finish(nc, x_tok, out_hbm, sd_x, n_)
                return nc

            with (
                nc.sbuf_tensor("lnstage", [128, 2 * D], BF16) as lnstage,
                nc.Block() as blk,
            ):
                V0, S0, T0 = n_["V"], n_["S"], n_["T"]
                pv, pl = {}, {}
                _c = 0
                _c += 1
                pv[0] = V0 + _c
                for t in range(T):
                    if t + 1 < T:
                        _c += 1
                        pv[t + 1] = V0 + _c
                    _c += 1
                    pl[t] = V0 + _c

                def pos_var(t):
                    return pv[t]

                def pos_ln(t):
                    return pl[t]

                tl_after = {}
                tlc = [0, 0]
                for t in range(T):
                    tlc[t % 2] += 16
                    tl_after[t] = tlc[t % 2]

                def xwait(v, t):
                    i2 = t // 2
                    sem = sd_xa if t % 2 == 0 else sd_xb
                    v.wait_ge(sem[i2 % 4], 16 * (i2 // 4 + 1))

                @blk.vector
                def _(v):
                    def stats(t):
                        xt = x_tok.ap()[:, t * D:(t + 1) * D]
                        xwait(v, t)
                        v.drain()
                        v.bn_stats(bnst[:, 0:6], xt[:, 0:512])
                        v.bn_stats(bnst[:, 6:12], xt[:, 512:1024])
                        v.drain()
                        v.bn_aggr(muvar.ap()[:, 2 * t:2 * t + 2], bnst[:, :]
                                  ).then_inc(s_V, 1)
                    stats(0)
                    for t in range(T):
                        if t + 1 < T:
                            stats(t + 1)
                        xt = x_tok.ap()[:, t * D:(t + 1) * D]
                        v.wait_ge(s_S, S0 + t + 1)
                        v.drain()
                        v.reciprocal(rstd_all.ap()[:, t:t + 1],
                                     sqv.ap()[:, t:t + 1])
                        v.drain()
                        v.tensor_scalar(tmp_c[:, :],
                                        muvar.ap()[:, 2 * t:2 * t + 1],
                                        rstd_all.ap()[:, t:t + 1], -1.0,
                                        OP.mult, OP.mult)
                        v.drain()
                        if t >= 2:
                            v.wait_ge(sd_tl[t % 2], tl_after[t - 2])
                        v.tensor_scalar(
                            ln_tile.ap()[:, (t % 2) * D:(t % 2 + 1) * D], xt,
                            rstd_all.ap()[:, t:t + 1], tmp_c[:, :],
                            OP.mult, OP.add).then_inc(s_V, 1)
                n_["V"] += 2 * T

                @blk.scalar
                def _(s):
                    for t in range(T):
                        s.wait_ge(s_V, pos_var(t))
                        s.activation(sqv.ap()[:, t:t + 1],
                                     muvar.ap()[:, 2 * t + 1:2 * t + 2],
                                     AF.Sqrt, bias=eps_r[:, :], scale=1.0
                                     ).then_inc(s_S, 1)
                n_["S"] += T

                @blk.sync
                def _(sync):
                    for t in range(T):
                        sync.wait_ge(s_V, pos_ln(t))
                        if t >= 2:
                            sync.wait_ge(sd_ts[t % 2], 16 * (t // 2))
                        sync.dma_start_transpose(
                            lnstage.ap()[:, (t % 2) * D:(t % 2 + 1) * D
                                         ].rearrange("p (j c) -> p j c", c=128),
                            ln_tile.ap()[:, (t % 2) * D:(t % 2 + 1) * D
                                         ].rearrange("p (j c) -> p j c", c=128)
                            ).then_inc(sd_tl[t % 2], 16)
                        sync.wait_ge(sd_tl[t % 2], 16 * (t // 2 + 1))
                        sync.dma_start(
                            out=ln1T.ap().rearrange("p (j s) -> p j s",
                                                    s=SEQ)[:, :,
                                                           t * 128:(t + 1) * 128],
                            in_=lnstage.ap()[:, (t % 2) * D:(t % 2 + 1) * D
                                             ].rearrange("p (j c) -> p j c",
                                                         c=128)
                            ).then_inc(sd_ts[t % 2], 16)

            if STOP == "p1":
                _finish(nc, x_tok, out_hbm, sd_x, n_)
                return nc

            with (
                nc.sbuf_tensor("qk_fm", [128, SEQ], F32) as qk_fm,
                nc.sbuf_tensor("v_fm", [64, SEQ], F32) as v_fm,
                nc.psum_tensor("p_qkv", [128, SEQ], F32) as p_qkv,
                nc.psum_tensor("p_t2", [128, 1024], F32) as p_t2,
                nc.psum_tensor("p_sm", [1, 128], F32) as p_sm,
                nc.psum_tensor("p_sm2", [128, 64], F32) as p_sm2,
                nc.Block() as blk,
            ):
                T0, V0 = n_["T"], n_["V"]

                @blk.tensor
                def _(tensor):
                    tensor.wait_ge(s_V, v_after_p0)
                    for piece in range(4):
                        sl = slice(piece * 512, (piece + 1) * 512)
                        tensor.wait_ge(sd_ts[0], 16 * (2 * piece + 2))
                        tensor.wait_ge(sd_ts[1], 16 * (2 * piece + 2))
                        for kc in range(C):
                            mm = tensor.matmul(
                                p_qkv[:, sl],
                                wqkvT.ap()[:, kc * 192:kc * 192 + 128],
                                ln1T.ap()[:, kc * SEQ + piece * 512:
                                          kc * SEQ + (piece + 1) * 512],
                                start=(kc == 0), stop=(kc == C - 1))
                        mm.then_inc(s_T, 1)
                    for piece in range(4):
                        sl = slice(piece * 512, (piece + 1) * 512)
                        tensor.wait_ge(s_V, V0 + piece + 1)
                        for kc in range(C):
                            mm = tensor.matmul(
                                p_qkv[0:64, sl],
                                wqkvT.ap()[:, kc * 192 + 128:kc * 192 + 192],
                                ln1T.ap()[:, kc * SEQ + piece * 512:
                                          kc * SEQ + (piece + 1) * 512],
                                start=(kc == 0), stop=(kc == C - 1))
                        mm.then_inc(s_T, 1)
                    for i in range(32):
                        jc = i % 16
                        slot = (i % 2) * 512
                        if i == 0:
                            tensor.wait_ge(s_V, V0 + 4)
                        if i == 16:
                            tensor.wait_ge(s_V, V0 + 8)
                        if i >= 2:
                            tensor.wait_ge(s_V, V0 + 8 + (i - 2) + 1)
                        if i < 16:
                            tensor.transpose(p_t2[:, slot:slot + 64],
                                             qk_fm.ap()[64:128,
                                                        jc * 128:(jc + 1) * 128],
                                             idf[64:128, 64:128]).then_inc(s_T, 1)
                        else:
                            tensor.transpose(p_t2[:, slot:slot + 64],
                                             v_fm.ap()[:, jc * 128:(jc + 1) * 128],
                                             idf[0:64, 0:64]).then_inc(s_T, 1)
                    tensor.wait_ge(s_V, V0 + 41)
                    tensor.transpose(p_sm[0:1, 0:64], tmp_c[0:64, 0:1],
                                     idf[0:64, 0:64]).then_inc(s_T, 1)
                    tensor.wait_ge(s_V, V0 + 42)
                    tensor.matmul(p_sm2[:, :], ones1[:, :], negm_row[:, :],
                                  start=True, stop=True).then_inc(s_T, 1)
                n_["T"] += 4 + 4 + 32 + 2

                @blk.vector
                def _(v):
                    v.wait_ge(sd_ts[0], 16 * (T // 2))
                    v.wait_ge(sd_ts[1], 16 * (T // 2))
                    for piece in range(4):
                        sl = slice(piece * 512, (piece + 1) * 512)
                        v.wait_ge(s_T, T0 + piece + 1)
                        v.tensor_copy(qk_fm.ap()[:, sl], p_qkv[:, sl])
                        v.tensor_copy(qb.ap()[:, sl], p_qkv[0:64, sl]
                                      ).then_inc(s_V, 1)
                    for piece in range(4):
                        sl = slice(piece * 512, (piece + 1) * 512)
                        v.wait_ge(s_T, T0 + 4 + piece + 1)
                        v.tensor_copy(v_fm.ap()[:, sl], p_qkv[0:64, sl]
                                      ).then_inc(s_V, 1)
                    for i in range(32):
                        jc = i % 16
                        slot = (i % 2) * 512
                        v.wait_ge(s_T, T0 + 8 + i + 1)
                        if i < 16:
                            v.tensor_scalar(kT.ap()[:, jc * 64:(jc + 1) * 64],
                                            p_t2[:, slot:slot + 64], 0.125,
                                            None, OP.mult).then_inc(s_V, 1)
                        else:
                            v.tensor_copy(
                                vT_p.ap()[:, jc * 128 + 1:jc * 128 + 128:2],
                                p_t2[:, slot:slot + 64]).then_inc(s_V, 1)
                    v.drain()
                    v.tensor_reduce(tmp_a[0:64, :], qk_fm.ap()[0:64, :], AX,
                                    OP.max, apply_absolute_value=True)
                    v.tensor_reduce(tmp_b[0:64, :], qk_fm.ap()[64:128, :], AX,
                                    OP.max, apply_absolute_value=True)
                    v.drain()
                    v.tensor_mul(tmp_c[0:64, :], tmp_a[0:64, :], tmp_b[0:64, :])
                    v.drain()
                    v.tensor_scalar(tmp_c[0:64, :], tmp_c[0:64, :], -0.125,
                                    None, OP.mult).then_inc(s_V, 1)
                    v.wait_ge(s_T, T0 + 41)
                    v.tensor_copy(negm_row[:, :], p_sm[0:1, 0:64]
                                  ).then_inc(s_V, 1)
                    v.wait_ge(s_T, T0 + 42)
                    v.tensor_copy(negm[:, :], p_sm2[:, :]).then_inc(s_V, 1)
                n_["V"] += 8 + 32 + 3

            if STOP == "p2":
                _finish(nc, x_tok, out_hbm, sd_x, n_)
                return nc

            v_after_p2 = n_["V"]
            heads = _attn_heads()
            T0, S0, V0, DX0 = n_["T"], n_["S"], n_["V"], n_["dx"]
            qrep_done, nd_chunk_done, nd_head_done = {}, {}, {}
            exp_pos, exp_head_done, mask_pos, pairv_pos = {}, {}, {}, {}
            tc = T0 + 2 + 2
            qrep_done[0] = T0 + 2
            qrep_done[1] = T0 + 4
            k = 0
            for (nn2, ih, h, chunks) in heads:
                for (jc, i_lo, diag) in chunks:
                    tc += len(_nd_pieces(i_lo))
                    nd_chunk_done[k] = tc
                    k += 1
                nd_head_done[nn2] = tc
                if nn2 + 2 < NHEADS:
                    tc += 2
                    qrep_done[nn2 + 2] = tc
            k = 0
            for (nn2, ih, h, chunks) in heads:
                for _ in chunks:
                    exp_pos[k] = S0 + k + 1
                    k += 1
                exp_head_done[nn2] = S0 + k
            vc = V0
            k = 0
            for (nn2, ih, h, chunks) in heads:
                if nn2 >= 1:
                    vc += 1
                    pairv_pos[nn2 - 1] = vc
                for (jc, i_lo, diag) in chunks:
                    if diag:
                        vc += 1
                        mask_pos[k] = vc
                    k += 1
            vc += 1
            pairv_pos[NHEADS - 1] = vc
            pair_dma = {m: 32 * (m // 2 + 1) for m in range(NHEADS)}
            sd_p = (sd_p0, sd_p1)
            NCHUNK = k

            wblocks = []
            for i_ in range(IC):
                wblocks.append(("g", i_))
                wblocks.append(("u", i_))
            for i_ in range(C):
                wblocks.append(("d", i_))
            NWB = len(wblocks)

            def wb_ntiles(p):
                return IC if wblocks[p][0] == "d" else C

            def w_off(p):
                return 2 * D if wblocks[p][0] == "d" else (p % 2) * D

            wt_after, wb_after = {}, {}
            _wtc, _wbc = [0, 0], [0, 0]
            for p in range(NWB):
                _wtc[p % 2] += 16
                wt_after[p] = _wtc[p % 2]
                _wbc[p % 2] += 16
                wb_after[p] = _wbc[p % 2]
            n_["wb0"], n_["wb1"] = _wbc[0], _wbc[1]
            heada_of, raw_head, piece_heads = {}, {}, {}
            for p in range(NWB):
                if wblocks[p][0] == "d":
                    nd_ = p - (2 * IC)
                    heada_of[p] = 4 * IC + 4 * nd_
                    piece_heads[p] = [heada_of[p] + 1 + i for i in range(3)]
                    raw_head[p] = heada_of[p]
                else:
                    heada_of[p] = 2 * p
                    piece_heads[p] = [2 * p + 1]
                    raw_head[p] = 0 if p < 2 else 2 * p
            xbar_head = {p: piece_heads[p][-1] + 1 for p in range(NWB)}
            wb_head = {p: xbar_head[p] + 1 for p in range(NWB)}
            quanta_at, quantb_at = {}, {}
            raw_at, xbar_at, wb_at = {}, {}, {}
            for p in range(NWB):
                quanta_at.setdefault(heada_of[p], []).append(p)
                wid_ = II if wblocks[p][0] == "d" else D
                offs = list(range(0, wid_, 1024))
                for i, off in enumerate(offs):
                    quantb_at.setdefault(piece_heads[p][i], []).append(
                        (p, off, i == 0, i == len(offs) - 1))
                raw_at.setdefault(raw_head[p], []).append(p)
                xbar_at.setdefault(xbar_head[p], []).append(p)
                wb_at.setdefault(wb_head[p], []).append(p)

            def emit_wraw(sc, p):
                kind, wi = wblocks[p]
                hh2 = p % 2
                wid = II if kind == "d" else D
                if kind == "d" and p >= 1 and wblocks[p - 1][0] == "d":
                    sc.wait_ge(s_wq, p)
                elif p >= 2:
                    sc.wait_ge(s_wq, p - 1)
                srcw = {"g": wg_hbm, "u": wu_hbm, "d": wd_hbm}[kind]
                obase = w_off(p)
                sc.dma_start(out=wrawW.ap()[:, obase:obase + wid],
                             in_=srcw[wi * 128:(wi + 1) * 128, 0:wid]
                             ).then_inc(sd_wr[hh2], 16)

            def emit_wquant_a(v, p):
                kind, wi = wblocks[p]
                hh2 = p % 2
                wid = II if kind == "d" else D
                rbase = w_off(p)
                v.wait_ge(sd_wr[hh2], 16 * (p // 2 + 1))
                v.drain()
                v.tensor_reduce(wscW.ap()[:, 0:1],
                                wrawW.ap()[:, rbase:rbase + wid], AX,
                                OP.max, apply_absolute_value=True)
                v.drain()
                v.tensor_scalar(wscW.ap()[:, 0:1], wscW.ap()[:, 0:1],
                                1.0 / 127.0, None, OP.mult)
                v.drain()
                v.reciprocal(wscW.ap()[:, 1:2], wscW.ap()[:, 0:1])
                v.drain()

            def emit_wquant_b(v, p, off, first, last):
                kind, wi = wblocks[p]
                wid = II if kind == "d" else D
                rbase = w_off(p)
                if first:
                    if kind == "d":
                        if wblocks[p - 1][0] == "d":
                            v.wait_ge(sd_wt[(p - 1) % 2], wt_after[p - 1])
                    elif p >= 2:
                        v.wait_ge(sd_wt[p % 2], wt_after[p - 2])
                w2 = min(1024, wid - off)
                v.tensor_scalar(scr1[:, 0:w2],
                                wrawW.ap()[:, rbase + off:rbase + off + w2],
                                wscW.ap()[:, 1:2], MAGIC,
                                OP.mult, OP.add)
                v.drain()
                ts = v.tensor_scalar(
                    wqW.ap()[:, rbase + off:rbase + off + w2],
                    scr1[:, 0:w2], MAGIC,
                    wscW.ap()[:, 0:1], OP.subtract, OP.mult)
                if last:
                    ts.then_inc(s_wq, 1)
                v.drain()

            def emit_wwb(sync, p):
                kind, wi = wblocks[p]
                hh2 = p % 2
                ob = w_off(p)
                sync.wait_ge(sd_wt[hh2], wt_after[p])
                if kind == "g":
                    sync.dma_start(out=wgT_hbm.ap()[:, wi * D:(wi + 1) * D],
                                   in_=wtW.ap()[:, ob:ob + D]
                                   ).then_inc(sd_wb[hh2], 16)
                elif kind == "u":
                    sync.dma_start(out=wuT_hbm.ap()[:, wi * D:(wi + 1) * D],
                                   in_=wtW.ap()[:, ob:ob + D]
                                   ).then_inc(sd_wb[hh2], 16)
                else:
                    sync.dma_start(
                        out=wdT_hbm.ap().rearrange(
                            "p (j c) -> p j c",
                            c=D)[:, :, wi * 128:(wi + 1) * 128],
                        in_=wtW.ap()[:, ob:ob + IC * 128].rearrange(
                            "p (j c) -> p j c", c=128)
                        ).then_inc(sd_wb[hh2], 16)

            def emit_wxbar(sync, p):
                kind, wi = wblocks[p]
                hh2 = p % 2
                ob = w_off(p)
                nt = wb_ntiles(p)
                sync.wait_ge(s_wq, p + 1)
                if kind == "d":
                    if wblocks[p - 1][0] == "d":
                        sync.wait_ge(sd_wb[(p - 1) % 2], wb_after[p - 1])
                elif p >= 2:
                    sync.wait_ge(sd_wb[hh2], wb_after[p - 2])
                sync.dma_start_transpose(
                    wtW.ap()[:, ob:ob + nt * 128].rearrange(
                        "p (j c) -> p j c", c=128),
                    wqW.ap()[:, ob:ob + nt * 128].rearrange(
                        "p (j c) -> p j c", c=128)
                    ).then_inc(sd_wt[hh2], 16)

            with (
                nc.sbuf_tensor("wrawW", [128, 2 * D + II], F32) as wrawW,
                nc.sbuf_tensor("wqW", [128, 2 * D + II], BF16) as wqW,
                nc.sbuf_tensor("wtW", [128, 2 * D + II], BF16) as wtW,
                nc.sbuf_tensor("wscW", [128, 2], F32) as wscW,
                nc.psum_tensor("p_qr0", [128, HALF], F32) as p_qr0,
                nc.psum_tensor("p_qr1", [128, HALF], F32) as p_qr1,
                nc.psum_tensor("p_nd0", [2, HALF], F32) as p_nd0,
                nc.psum_tensor("p_nd1", [2, HALF], F32) as p_nd1,
                nc.Block() as blk,
            ):
                p_qr = (p_qr0, p_qr1)
                p_nd = (p_nd0, p_nd1)

                def emit_qrep(tensor, m):
                    _, ihm, hm, _ = heads[m]
                    if m >= 2:
                        tensor.wait_ge(s_S, exp_head_done[m - 2])
                    for half2 in range(2):
                        tensor.matmul(
                            p_qr[m % 2][:, half2 * 512:(half2 + 1) * 512],
                            sel.ap()[:, hm * 128:(hm + 1) * 128],
                            qb.ap()[:, ihm * HALF + half2 * 512:
                                    ihm * HALF + (half2 + 1) * 512],
                            start=True, stop=True).then_inc(s_T, 1)

                @blk.tensor
                def _(tensor):
                    tensor.wait_ge(s_V, v_after_p2)
                    emit_qrep(tensor, 0)
                    emit_qrep(tensor, 1)
                    kk = 0
                    for (m, ih, h, chunks) in heads:
                        for ci, (jc, i_lo, diag) in enumerate(chunks):
                            if ci == 0 and m >= 2:
                                tensor.wait_ge(s_V, pairv_pos[m - 2])
                            if diag:
                                tensor.wait_ge(s_V, mask_pos[kk])
                            else:
                                tensor.wait_ge(s_S, exp_pos[kk])
                            par = kk % 6
                            last_ci = {}
                            for ci2, (_, il2, _) in enumerate(chunks):
                                for (p2, _, _) in _nd_pieces(il2):
                                    last_ci[p2] = ci2
                            for (p, lo, hi) in _nd_pieces(i_lo):
                                tensor.matmul(
                                    p_nd[m % 2][0:2, lo:hi],
                                    vT_p.ap()[:, jc * 128 + 2 * h:
                                              jc * 128 + 2 * h + 2],
                                    E_buf.ap()[:, par * HALF + lo:
                                               par * HALF + hi],
                                    start=(ci == 0),
                                    stop=(ci == last_ci[p])).then_inc(s_T, 1)
                            kk += 1
                        if m + 2 < NHEADS:
                            emit_qrep(tensor, m + 2)
                n_["T"] = tc

                @blk.scalar
                def _(s):
                    s.wait_ge(s_V, v_after_p2)
                    kk = 0
                    for (m, ih, h, chunks) in heads:
                        for p_ in raw_at.get(m, ()):
                            emit_wraw(s, p_)
                        for (jc, i_lo, diag) in chunks:
                            par = kk % 6
                            w = qrep_done[m]
                            if kk >= 6:
                                w = max(w, nd_chunk_done[kk - 6])
                            s.wait_ge(s_T, w)
                            s.activation(
                                E_buf.ap()[:, par * HALF + i_lo:(par + 1) * HALF],
                                p_qr[m % 2][:, i_lo:HALF],
                                AF.Exp,
                                bias=negm.ap()[:, h:h + 1],
                                scale=kT.ap()[:, jc * 64 + h:jc * 64 + h + 1]
                                ).then_inc(s_S, 1)
                            kk += 1
                n_["S"] += NCHUNK

                @blk.vector
                def _(v):
                    kk = 0
                    for (m, ih, h, chunks) in heads:
                        if m >= 1:
                            mp = m - 1
                            v.wait_ge(s_T, nd_head_done[mp])
                            if mp >= 2:
                                v.wait_ge(sd_p[mp % 2], pair_dma[mp - 2])
                            v.tensor_copy(
                                pair_sb.ap()[:, (mp % 2) * HALF:
                                             (mp % 2 + 1) * HALF],
                                p_nd[mp % 2][:, :]).then_inc(s_V, 1)
                        for (jc, i_lo, diag) in chunks:
                            if diag:
                                par = kk % 6
                                v.wait_ge(s_S, exp_pos[kk])
                                v.tensor_mul(
                                    E_buf.ap()[:, par * HALF + i_lo:
                                               par * HALF + i_lo + 128],
                                    E_buf.ap()[:, par * HALF + i_lo:
                                               par * HALF + i_lo + 128],
                                    tri[:, :]).then_inc(s_V, 1)
                            kk += 1
                        for p_ in quanta_at.get(m, ()):
                            emit_wquant_a(v, p_)
                        for (p_, off_, fi_, la_) in quantb_at.get(m, ()):
                            emit_wquant_b(v, p_, off_, fi_, la_)
                    mp = NHEADS - 1
                    v.wait_ge(s_T, nd_head_done[mp])
                    v.wait_ge(sd_p[mp % 2], pair_dma[mp - 2])
                    v.tensor_copy(pair_sb.ap()[:, (mp % 2) * HALF:
                                               (mp % 2 + 1) * HALF],
                                  p_nd[mp % 2][:, :]).then_inc(s_V, 1)
                n_["V"] = vc

                @blk.sync
                def _(sync):
                    for (m, ih, h, chunks) in heads:
                        for p_ in wb_at.get(m, ()):
                            emit_wwb(sync, p_)
                        for p_ in xbar_at.get(m, ()):
                            emit_wxbar(sync, p_)
                        sync.wait_ge(s_V, pairv_pos[m])
                        if m == NHEADS - 1:
                            sync.wait_ge(sd_wt[0], _wtc[0])
                            sync.wait_ge(sd_wt[1], _wtc[1])
                            sync.wait_ge(sd_wb[0], _wbc[0])
                            sync.wait_ge(sd_wb[1], _wbc[1])
                        sec = (m % 2) * HALF
                        sync.dma_start(
                            out=da_sb.ap()[h:h + 1, ih * HALF:(ih + 1) * HALF],
                            in_=pair_sb.ap()[0:1, sec:sec + HALF]
                            ).then_inc(sd_p[m % 2], 16)
                        sync.dma_start(
                            out=na_sb.ap()[h:h + 1, ih * HALF:(ih + 1) * HALF],
                            in_=pair_sb.ap()[1:2, sec:sec + HALF]
                            ).then_inc(sd_p[m % 2], 16)

            nc.all_engine_barrier()
            if STOP == "p3":
                _finish(nc, x_tok, out_hbm, sd_x, n_)
                return nc

            with (
                nc.psum_tensor("p_w", [128, 1536], F32) as p_w,
                nc.Block() as blk,
            ):
                T0, V0 = n_["T"], n_["V"]

                @blk.vector
                def _(v):
                    v.wait_ge(sd_p0, 32 * (NHEADS // 2))
                    v.wait_ge(sd_p1, 32 * (NHEADS // 2))
                    v.reciprocal(da_sb[:, :], da_sb[:, :])
                    v.drain()
                    v.tensor_mul(attn_o[:, :], na_sb[:, :], da_sb[:, :]
                                 ).then_inc(s_V, 1)
                    for idx in range(32):
                        t, piece = divmod(idx, 2)
                        v.wait_ge(s_T, T0 + idx + 1)
                        sl = slice((idx % 3) * 512, (idx % 3) * 512 + 512)
                        xsl = slice(t * D + piece * 512, t * D + piece * 512 + 512)
                        v.tensor_add(x_tok.ap()[:, xsl], x_tok.ap()[:, xsl],
                                     p_w[:, sl]).then_inc(s_V, 1)
                n_["V"] += 33

                @blk.tensor
                def _(tensor):
                    tensor.wait_ge(s_V, V0 + 1)
                    for idx in range(32):
                        t, piece = divmod(idx, 2)
                        if idx >= 3:
                            tensor.wait_ge(s_V, V0 + 1 + (idx - 3) + 1)
                        sl = slice((idx % 3) * 512, (idx % 3) * 512 + 512)
                        tensor.matmul(p_w[:, sl],
                                      attn_o.ap()[:, t * 128:(t + 1) * 128],
                                      woutT.ap()[:, piece * 512:(piece + 1) * 512],
                                      start=True, stop=True).then_inc(s_T, 1)
                n_["T"] += 32

            nc.all_engine_barrier()
            if STOP == "p3b":
                _finish(nc, x_tok, out_hbm, sd_x, n_)
                return nc

        with nc.Block() as blk:
            V0, S0 = n_["V"], n_["S"]

            @blk.vector
            def _(v):
                for t in range(T):
                    xt = x_tok.ap()[:, t * D:(t + 1) * D]
                    v.drain()
                    v.bn_stats(bnst[:, 0:6], xt[:, 0:512])
                    v.bn_stats(bnst[:, 6:12], xt[:, 512:1024])
                    v.tensor_reduce(xmax_a.ap()[:, t:t + 1], xt, AX, OP.max)
                    v.tensor_reduce(xmin_a.ap()[:, t:t + 1], xt, AX, OP.min,
                                    negate=False)
                    v.drain()
                    v.bn_aggr(muvar.ap()[:, 2 * t:2 * t + 2], bnst[:, :]
                              ).then_inc(s_V, 1)
                mu_v = muvar.ap().rearrange("p (t k) -> p t k", k=2)[:, :, 0:1]
                v.wait_ge(s_S, S0 + 1)
                v.drain()
                v.reciprocal(rstd_all[:, 0:T], sqv[:, :])
                v.tensor_sub(xmax_a[:, :], xmax_a[:, :], mu_v)
                v.tensor_sub(dev_b[:, :], mu_v, xmin_a[:, :])
                v.drain()
                v.tensor_max(xmax_a[:, :], xmax_a[:, :], dev_b[:, :])
                v.drain()
                v.tensor_mul(xmax_a[:, :], xmax_a[:, :], rstd_all[:, 0:T])
                v.drain()
                v.drain()
                v.tensor_reduce(amax_run[:, :], xmax_a[:, :], AX, OP.max
                                ).then_inc(s_V, 1)
            n_["V"] += T + 1

            @blk.scalar
            def _(s):
                s.wait_ge(s_V, V0 + T)
                var_v = muvar.ap().rearrange("p (t k) -> p t k", k=2)[:, :, 1:2]
                s.activation(sqv[:, :], var_v, AF.Sqrt,
                             bias=eps_r[:, :], scale=1.0).then_inc(s_S, 1)
            n_["S"] += 1

        nc.all_engine_barrier()

        with ExitStack() as fctx:
            fn = fctx.enter_context
            hpreT = fn(nc.sbuf_tensor("hpreT", [128, IC * SEQ], BF16))
            cc_stage = fn(nc.sbuf_tensor("cc_stage", [1, 1], F32))
            amax2 = fn(nc.sbuf_tensor("amax2", [128, 1], F32))
            scrA = fn(nc.sbuf_tensor("scrA", [128, D], F32))
            xstage = fn(nc.sbuf_tensor("xstage", [128, 2 * D], BF16))
            scrB = fn(nc.sbuf_tensor("scrB", [128, 2 * D], F32))

            if STOP == "exch1":
                _finish(nc, x_tok, out_hbm, sd_x, n_)
                return nc

            with ExitStack() as f2:
                f2n = f2.enter_context
                xqT = f2n(nc.sbuf_tensor("xqT", [128, C * SEQ], BF16))
                wgq = f2n(nc.sbuf_tensor("wgq", [128, D], BF16))
                wuq = f2n(nc.sbuf_tensor("wuq", [128, D], BF16))
                wgT = f2n(nc.sbuf_tensor("wgT", [128, 2 * D], BF16))
                wuT = f2n(nc.sbuf_tensor("wuT", [128, 2 * D], BF16))
                silu_sb = f2n(nc.sbuf_tensor("silu_sb", [128, 1536], BF16))

                with (
                    nc.psum_tensor("p_sm4b", [1, 128], F32) as p_sm4b,
                    nc.Block() as blk,
                ):
                    V0, S0, T0 = n_["V"], n_["S"], n_["T"]

                    @blk.tensor
                    def _(tensor):
                        tensor.transpose(p_sm4b[0:1, :], amax_run[:, 0:1], idf[:, :]
                                         ).then_inc(s_T, 1)
                    n_["T"] += 1

                    @blk.vector
                    def _(v):
                        v.memset(cc_stage[:, :], 0.0)
                        v.drain()
                        v.wait_ge(s_T, T0 + 1)
                        v.tensor_reduce(cc_stage.ap()[0:1, 0:1], p_sm4b[0:1, :],
                                        AX, OP.max).then_inc(s_V, 1)
                    n_["V"] += 1

                    @blk.scalar
                    def _(s):
                        s.wait_ge(s_V, V0 + 1)
                        s.dma_start(out=cc_in1.ap()[0:1, 0:1],
                                    in_=cc_stage[:, :]).then_inc(sd_w, 16)
                        s.activation(tmp_b[:, :], clipx_r[:, :], AF.Sigmoid
                                     ).then_inc(s_S, 1)
                        s.wait_ge(s_cc, 1)
                        s.dma_start(out=gmax_r[:, :],
                                    in_=cc_out1.ap()[0:1, 0:1].to_broadcast([128, 1])
                                    ).then_inc(sd_w, 16)
                    n_["S"] += 1

                    @blk.gpsimd
                    def _(g):
                        g.wait_ge(sd_w, n_["dw"] + 16)
                        g.collective_compute(
                            "AllReduce", OP.max,
                            replica_groups=[list(range(NCORES))],
                            ins=[cc_in1.ap().opt()], outs=[cc_out1.ap().opt()],
                            ).then_inc(s_cc, 1)
                    n_["dw"] += 32
                    n_["cc"] += 1

                    @blk.vector
                    def _(v):
                        v.wait_ge(sd_w, n_["dw"])
                        v.wait_ge(s_S, S0 + 1)
                        v.tensor_mul(sc_sx[:, :], tmp_b[:, :], gmax_r[:, :])
                        v.drain()
                        v.tensor_scalar(sc_sx[:, :], sc_sx[:, :], 1.0 / 127.0,
                                        None, OP.mult)
                        v.drain()
                        v.reciprocal(sc_rx[:, :], sc_sx[:, :])
                        v.drain()
                        v.tensor_scalar(amat.ap()[:, 0:T], rstd_all[:, 0:T],
                                        sc_rx.ap()[:, 0:1], None, OP.mult)
                        v.drain()
                        mu_v = muvar.ap().rearrange("p (t k) -> p t k",
                                                    k=2)[:, :, 0:1]
                        v.tensor_mul(bmat.ap()[:, 0:T], mu_v, amat.ap()[:, 0:T])
                        v.drain()
                        v.tensor_scalar(bmat.ap()[:, 0:T], bmat.ap()[:, 0:T],
                                        -1.0, None, OP.mult).then_inc(s_V, 1)
                    n_["V"] += 1

                nc.all_engine_barrier()

                with nc.Block() as blk:
                    V0, T0, S0 = n_["V"], n_["T"], n_["S"]
                    pos_q = {t: V0 + t + 1 for t in range(T)}
                    tq_after = {}
                    tqc = [0, 0]
                    for t in range(T):
                        tqc[t % 2] += 16
                        tq_after[t] = tqc[t % 2]

                    @blk.scalar
                    def _(s):
                        for t in range(T):
                            xt = x_tok.ap()[:, t * D:(t + 1) * D]
                            s.drain()
                            s.activation(scrA[:, :], xt, AF.Identity,
                                         scale=amat.ap()[:, t:t + 1],
                                         bias=bmat.ap()[:, t:t + 1])
                            if t >= 2:
                                s.wait_ge(s_V, pos_q[t - 2])
                            s.drain()
                            s.activation(scrB[:, (t % 2) * D:(t % 2 + 1) * D],
                                         scrA[:, :], AF.Identity,
                                         bias=magic_r[:, :], scale=1.0
                                         ).then_inc(s_S, 1)
                    n_["S"] += T

                    @blk.vector
                    def _(v):
                        for t in range(T):
                            wq = wgq if t % 2 == 0 else wuq
                            if t >= 2:
                                v.wait_ge(sd_tq[t % 2], tq_after[t - 2])
                            v.wait_ge(s_S, S0 + t + 1)
                            v.drain()
                            v.tensor_scalar(scr1[:, :],
                                            scrB[:, (t % 2) * D:(t % 2 + 1) * D],
                                            MAGIC, 127.0,
                                            OP.subtract, OP.min)
                            v.drain()
                            v.tensor_scalar(wq[:, :], scr1[:, :],
                                            -128.0, None, OP.max
                                            ).then_inc(s_V, 1)
                    n_["V"] += T

                    @blk.sync
                    def _(sync):
                        for t in range(T):
                            wq = wgq if t % 2 == 0 else wuq
                            sync.wait_ge(s_V, pos_q[t])
                            if t >= 2:
                                sync.wait_ge(sd_tx[t % 2], 16 * (t // 2))
                            sync.dma_start_transpose(
                                xstage.ap()[:, (t % 2) * D:(t % 2 + 1) * D
                                            ].rearrange("p (j c) -> p j c",
                                                        c=128),
                                wq.ap().rearrange("p (j c) -> p j c", c=128)
                                ).then_inc(sd_tq[t % 2], 16)
                            sync.wait_ge(sd_tq[t % 2], 16 * (t // 2 + 1))
                            sync.dma_start(
                                out=xqT.ap().rearrange(
                                    "p (j s) -> p j s",
                                    s=SEQ)[:, :, t * 128:(t + 1) * 128],
                                in_=xstage.ap()[:, (t % 2) * D:(t % 2 + 1) * D
                                                ].rearrange("p (j c) -> p j c",
                                                            c=128)
                                ).then_inc(sd_tx[t % 2], 16)
                        sync.wait_ge(sd_tx[0], 16 * (T // 2))
                        sync.wait_ge(sd_tx[1], 16 * (T // 2))

                nc.all_engine_barrier()

                T0, S0, V0 = n_["T"], n_["S"], n_["V"]
                dma_g = {mi: 16 * (mi // 2 + 1) for mi in range(IC)}
                dma_u = {mi: 16 * (mi // 2 + 1) for mi in range(IC)}
                pos_stt = {(mi, q4): V0 + (mi * 4 + q4) * 2 + 1
                           for mi in range(IC) for q4 in range(4)}
                pos_macc = {(mi, q4): V0 + (mi * 4 + q4) * 2 + 2
                            for mi in range(IC) for q4 in range(4)}
                vcn = V0 + IC * 8
                pos_gmm, pos_umm = {}, {}
                tcn = T0
                for mi_ in range(IC):
                    for q4 in range(4):
                        tcn += 1
                        pos_gmm[mi_, q4] = tcn
                        tcn += 1
                        pos_umm[mi_, q4] = tcn
                pos_silu = {}
                scn = S0
                for mi_ in range(IC):
                    for q4 in range(4):
                        scn += 1
                        pos_silu[mi_, q4] = scn

                with (
                    nc.psum_tensor("p_g", [128, 1536], F32) as p_g,
                    nc.psum_tensor("p_u", [128, 1536], F32) as p_u,
                    nc.Block() as blk,
                ):
                    @blk.scalar
                    def _(s):
                        s.wait_ge(sd_wb[0], n_["wb0"])
                        s.wait_ge(sd_wb[1], n_["wb1"])

                        def dma_pair(mi):
                            if mi >= 2:
                                s.wait_ge(s_T, pos_umm[mi - 2, 3])
                            s.dma_start(
                                out=wgT.ap()[:, (mi % 2) * D:(mi % 2 + 1) * D],
                                in_=wgT_hbm.ap()[:, mi * D:(mi + 1) * D]
                                ).then_inc(sd_wg[mi % 2], 16)
                            s.dma_start(
                                out=wuT.ap()[:, (mi % 2) * D:(mi % 2 + 1) * D],
                                in_=wuT_hbm.ap()[:, mi * D:(mi + 1) * D]
                                ).then_inc(sd_wu[mi % 2], 16)
                        dma_pair(0)
                        dma_pair(1)
                        for mi in range(IC):
                            for q4 in range(4):
                                gq = mi * 4 + q4
                                sl = slice((gq % 3) * 512, (gq % 3) * 512 + 512)
                                if gq >= 3:
                                    m2, q2 = divmod(gq - 3, 4)
                                    s.wait_ge(s_V, pos_stt[m2, q2])
                                s.wait_ge(s_T, pos_gmm[mi, q4])
                                s.activation(silu_sb[:, sl], p_g[:, sl],
                                             AF.Silu, scale=sc_sx.ap()[:, 0:1]
                                             ).then_inc(s_S, 1)
                            if mi + 2 < IC:
                                dma_pair(mi + 2)
                    n_["S"] = scn

                    @blk.vector
                    def _(v):
                        def epi(m):
                            for q4 in range(4):
                                gq = m * 4 + q4
                                sl = slice((gq % 3) * 512, (gq % 3) * 512 + 512)
                                v.wait_ge(s_S, pos_silu[m, q4])
                                v.wait_ge(s_T, pos_umm[m, q4])
                                hsl = slice(m * SEQ + q4 * 512,
                                            m * SEQ + (q4 + 1) * 512)
                                v.scalar_tensor_tensor(
                                    hpreT.ap()[:, hsl], p_u[:, sl],
                                    sc_sx.ap()[:, 0:1], silu_sb[:, sl],
                                    OP.mult, OP.mult).then_inc(s_V, 1)
                                v.drain()
                                v.tensor_reduce(tmp_e[:, :], hpreT.ap()[:, hsl],
                                                AX, OP.max,
                                                apply_absolute_value=True)
                                v.drain()
                                if m == 0 and q4 == 0:
                                    v.tensor_copy(amax2[:, :], tmp_e[:, :]
                                                  ).then_inc(s_V, 1)
                                else:
                                    v.tensor_max(amax2[:, :], amax2[:, :],
                                                 tmp_e[:, :]).then_inc(s_V, 1)
                        for mi in range(IC):
                            epi(mi)
                    n_["V"] = vcn

                    @blk.tensor
                    def _(tensor):
                        for mi in range(IC):
                            tensor.wait_ge(sd_wg[mi % 2], dma_g[mi])
                            tensor.wait_ge(sd_wu[mi % 2], dma_u[mi])
                            for q4 in range(4):
                                gq = mi * 4 + q4
                                sl = slice((gq % 3) * 512, (gq % 3) * 512 + 512)
                                if gq >= 3:
                                    m2, q2 = divmod(gq - 3, 4)
                                    tensor.wait_ge(s_S, pos_silu[m2, q2])
                                for kc in range(C):
                                    mm = tensor.matmul(
                                        p_g[:, sl],
                                        wgT.ap()[:, (mi % 2) * D + kc * 128:
                                                 (mi % 2) * D + (kc + 1) * 128],
                                        xqT.ap()[:, kc * SEQ + q4 * 512:
                                                 kc * SEQ + (q4 + 1) * 512],
                                        start=(kc == 0), stop=(kc == C - 1))
                                mm.then_inc(s_T, 1)
                                if gq >= 3:
                                    m2, q2 = divmod(gq - 3, 4)
                                    tensor.wait_ge(s_V, pos_stt[m2, q2])
                                for kc in range(C):
                                    mm = tensor.matmul(
                                        p_u[:, sl],
                                        wuT.ap()[:, (mi % 2) * D + kc * 128:
                                                 (mi % 2) * D + (kc + 1) * 128],
                                        xqT.ap()[:, kc * SEQ + q4 * 512:
                                                 kc * SEQ + (q4 + 1) * 512],
                                        start=(kc == 0), stop=(kc == C - 1))
                                mm.then_inc(s_T, 1)
                    n_["T"] = tcn

                nc.all_engine_barrier()
                if STOP == "p5":
                    _finish(nc, x_tok, out_hbm, sd_x, n_)
                    return nc

                nc.all_engine_barrier()


            with nc.sbuf_tensor("wdT", [128, IC * D], BF16) as wdT:
                with (
                    nc.psum_tensor("p_sm5", [1, 128], F32) as p_sm5,
                    nc.Block() as blk,
                ):
                    V0, S0, T0 = n_["V"], n_["S"], n_["T"]

                    @blk.tensor
                    def _(tensor):
                        tensor.transpose(p_sm5[0:1, :], amax2[:, 0:1], idf[:, :]
                                         ).then_inc(s_T, 1)
                    n_["T"] += 1

                    @blk.vector
                    def _(v):
                        v.memset(cc_stage[:, :], 0.0)
                        v.drain()
                        v.wait_ge(s_T, T0 + 1)
                        v.tensor_reduce(cc_stage.ap()[0:1, 0:1], p_sm5[0:1, :],
                                        AX, OP.max).then_inc(s_V, 1)
                    n_["V"] += 1

                    @blk.scalar
                    def _(s):
                        s.wait_ge(s_V, V0 + 1)
                        s.dma_start(out=cc_in2.ap()[0:1, 0:1],
                                    in_=cc_stage[:, :]).then_inc(sd_w, 16)
                        s.activation(tmp_b[:, :], clipd_r[:, :], AF.Sigmoid
                                     ).then_inc(s_S, 1)
                        for ic in range(IC):
                            s.dma_start(out=wdT.ap()[:, ic * D:(ic + 1) * D],
                                        in_=wdT_hbm.ap()[:, ic * D:(ic + 1) * D]
                                        ).then_inc(sd_wd, 16)
                        s.wait_ge(s_cc, 2)
                        s.dma_start(out=gmax_r[:, :],
                                    in_=cc_out2.ap()[0:1, 0:1].to_broadcast(
                                        [128, 1])).then_inc(sd_w, 16)
                    n_["S"] += 1

                    @blk.gpsimd
                    def _(g):
                        g.wait_ge(sd_w, n_["dw"] + 16)
                        g.collective_compute(
                            "AllReduce", OP.max,
                            replica_groups=[list(range(NCORES))],
                            ins=[cc_in2.ap().opt()], outs=[cc_out2.ap().opt()],
                            ).then_inc(s_cc, 1)
                    n_["dw"] += 32
                    n_["cc"] += 1

                    @blk.vector
                    def _(v):
                        v.wait_ge(sd_w, n_["dw"])
                        v.wait_ge(s_S, S0 + 1)
                        v.tensor_mul(sc_sh[:, :], tmp_b[:, :], gmax_r[:, :])
                        v.drain()
                        v.tensor_scalar(sc_sh[:, :], sc_sh[:, :], 1.0 / 127.0,
                                        None, OP.mult)
                        v.drain()
                        v.reciprocal(sc_rh[:, :], sc_sh[:, :]).then_inc(s_V, 1)
                    n_["V"] += 1


                nc.all_engine_barrier()
                with (
                    nc.psum_tensor("p_d", [128, 1536], F32) as p_d,
                    nc.Block() as blk,
                ):
                    T0, V0, S0 = n_["T"], n_["V"], n_["S"]
                    pos_qh, pos_add, pos_c1, pos_act = {}, {}, {}, {}
                    vcn2 = V0
                    for hh in range(2):
                        for ic in range(IC):
                            k_ = hh * IC + ic
                            vcn2 += 1
                            pos_c1[k_] = vcn2
                            vcn2 += 1
                            pos_qh[hh, ic] = vcn2
                        for idx in range(hh * 16, hh * 16 + 16):
                            vcn2 += 1
                            pos_add[idx] = vcn2
                    for k_ in range(2 * IC):
                        pos_act[k_] = S0 + k_ + 1
                    pos_dmm = {idx: T0 + idx + 1 for idx in range(32)}

                    @blk.scalar
                    def _(s):
                        for hh in range(2):
                            for ic in range(IC):
                                k_ = hh * IC + ic
                                sl = slice(ic * SEQ + hh * 1024,
                                           ic * SEQ + (hh + 1) * 1024)
                                if k_ >= 2:
                                    s.wait_ge(s_V, pos_c1[k_ - 2])
                                s.drain()
                                s.activation(scrB[:, (k_ % 2) * D:(k_ % 2 + 1) * D],
                                             hpreT.ap()[:, sl],
                                             AF.Identity,
                                             scale=sc_rh.ap()[:, 0:1],
                                             bias=magic_r[:, :]).then_inc(s_S, 1)
                    n_["S"] += 2 * IC

                    @blk.vector
                    def _(v):
                        for hh in range(2):
                            for ic in range(IC):
                                k_ = hh * IC + ic
                                sl = slice(ic * SEQ + hh * 1024,
                                           ic * SEQ + (hh + 1) * 1024)
                                v.wait_ge(s_S, pos_act[k_])
                                v.drain()
                                v.tensor_scalar(scr1[:, :],
                                                scrB[:, (k_ % 2) * D:(k_ % 2 + 1) * D],
                                                MAGIC, 127.0,
                                                OP.subtract, OP.min
                                                ).then_inc(s_V, 1)
                                v.drain()
                                v.tensor_scalar(hpreT.ap()[:, sl], scr1[:, :],
                                                -128.0, None, OP.max
                                                ).then_inc(s_V, 1)
                            for idx in range(hh * 16, hh * 16 + 16):
                                t, piece = divmod(idx, 2)
                                sl = slice((idx % 3) * 512, (idx % 3) * 512 + 512)
                                xsl = slice(t * D + piece * 512,
                                            t * D + piece * 512 + 512)
                                v.wait_ge(s_T, pos_dmm[idx])
                                v.scalar_tensor_tensor(
                                    x_tok.ap()[:, xsl], p_d[:, sl],
                                    sc_sh.ap()[:, 0:1], x_tok.ap()[:, xsl],
                                    OP.mult, OP.add).then_inc(s_V, 1)
                    n_["V"] = vcn2

                    @blk.tensor
                    def _(tensor):
                        tensor.wait_ge(sd_wd, 16 * IC)
                        for idx in range(32):
                            t, piece = divmod(idx, 2)
                            hh = t // 8
                            sl = slice((idx % 3) * 512, (idx % 3) * 512 + 512)
                            if idx == hh * 16:
                                tensor.wait_ge(s_V, pos_qh[hh, IC - 1])
                            if idx >= 3:
                                tensor.wait_ge(s_V, pos_add[idx - 3])
                            for ic in range(IC):
                                mm = tensor.matmul(
                                    p_d[:, sl],
                                    hpreT.ap()[:, ic * SEQ + t * 128:
                                               ic * SEQ + (t + 1) * 128],
                                    wdT.ap()[:, ic * D + piece * 512:
                                             ic * D + (piece + 1) * 512],
                                    start=(ic == 0), stop=(ic == IC - 1))
                            mm.then_inc(s_T, 1)
                    n_["T"] = T0 + 32

                    @blk.sync
                    def _(sync):
                        for t in range(T):
                            sync.wait_ge(s_V, pos_add[2 * t + 1])
                            sync.dma_start(
                                out=out_hbm[t * 128:(t + 1) * 128, :],
                                in_=x_tok.ap()[:, t * D:(t + 1) * D]
                                ).then_inc(sd_x, 16)
                        sync.wait_ge(sd_x, n_["dx"] + 16 * T)

    return nc


def kernel(L0_attn_x, L0_attn_w_qkv, L0_attn_w_out, L0_ffn_clip_x,
           L0_ffn_clip_down, L0_ffn_w_gate, L0_ffn_w_up, L0_ffn_w_down):
    x = np.ascontiguousarray(np.asarray(L0_attn_x, dtype=np.float32))
    nc = build_nc()
    common = dict(
        w_qkv=np.ascontiguousarray(np.asarray(L0_attn_w_qkv, np.float32)),
        w_out=np.ascontiguousarray(np.asarray(L0_attn_w_out, np.float32)),
        clip_x=np.ascontiguousarray(np.asarray(L0_ffn_clip_x, np.float32)),
        clip_d=np.ascontiguousarray(np.asarray(L0_ffn_clip_down, np.float32)),
        w_gate=np.ascontiguousarray(np.asarray(L0_ffn_w_gate, np.float32)),
        w_up=np.ascontiguousarray(np.asarray(L0_ffn_w_up, np.float32)),
        w_down=np.ascontiguousarray(np.asarray(L0_ffn_w_down, np.float32)),
    )
    in_maps = [dict(common, x=np.ascontiguousarray(x[b])) for b in range(NCORES)]
    res = run_bass_kernel_spmd(nc, in_maps, core_ids=list(range(NCORES)))
    return np.stack([res.results[b]["out"] for b in range(NCORES)], axis=0)
